# version 07
# baseline (speedup 1.0000x reference)
import math
import numpy as np
import ml_dtypes
from contextlib import ExitStack

import concourse.bass as bass
import concourse.mybir as mybir
from concourse.bass_utils import run_bass_kernel_spmd

F32 = mybir.dt.float32
BF16 = mybir.dt.bfloat16
AF = mybir.ActivationFunctionType
ALU = mybir.AluOpType
AXX = mybir.AxisListType.X

BF = ml_dtypes.bfloat16

D = 2048
S = 2048
NH = 16
HD = 128
HF = 8192
NC = 8
TT = 512
NTT = S // TT
SM_SCALE = 1.0 / math.sqrt(HD)
THR_C = 0.5 * (1.0 + 2.0 ** -9)
LN_EPS = 1e-5
RMS_EPS = float(np.finfo(np.float32).eps)

ENGINES = ("sync", "vector", "scalar", "tensor", "gpsimd")
SEMNAMES = (["Wg0", "Wg1", "Wu0", "Wu1"] + [f"Wp{i}" for i in range(8)]
            + ["Wl0", "Wl1", "Wl2", "Wl3", "Wq", "Wln", "Wz", "Wc", "Xt",
               "Go0", "Go1", "Gl0", "Gl1", "Gq", "Gd",
               "PE", "DV", "AC", "GP", "CC", "B"])
DMA_SEMS = tuple(n for n in SEMNAMES if n[0] in "WXG" and n != "GP")


class P:

    def __init__(self, sems):
        self.ops = {e: [] for e in ENGINES}
        self.sems = sems
        self.ctr = {name: 0 for name in sems}

    def op(self, engine, method, *args, inc=None, inc_n=1, **kwargs):
        val = None
        if inc is not None:
            self.ctr[inc] += inc_n
            val = self.ctr[inc]
            sem = self.sems[inc]

            def fn(eng, method=method, args=args, kwargs=kwargs, sem=sem, n=inc_n):
                getattr(eng, method)(*args, **kwargs).then_inc(sem, n)
        else:
            def fn(eng, method=method, args=args, kwargs=kwargs):
                getattr(eng, method)(*args, **kwargs)
        self.ops[engine].append(fn)
        return val

    def dma(self, engine, out, in_, sem):
        return self.op(engine, "dma_start", inc=sem, inc_n=16, out=out, in_=in_)

    def wait(self, engine, semname, val):
        if not val:
            return
        sem = self.sems[semname]

        def fn(eng, sem=sem, val=val):
            eng.wait_ge(sem, val)
        self.ops[engine].append(fn)

    def gate(self, engines=ENGINES, sems=None):
        if sems is None:
            sems = [n for n in SEMNAMES if n not in ("B", "CC")]
        snap = {k: self.ctr[k] for k in sems}
        own = {"sync": None, "vector": "DV", "scalar": "AC", "tensor": "PE", "gpsimd": "GP"}
        for e in engines:
            for k, v in snap.items():
                if v and own.get(e) != k:
                    self.wait(e, k, v)

    def barrier(self):
        self.ctr["B"] += len(ENGINES)
        target = self.ctr["B"]
        sem = self.sems["B"]
        for e in ENGINES:
            def fn(eng, sem=sem, target=target):
                eng.drain()
                eng.nop().then_inc(sem, 1)
                eng.wait_ge(sem, target)
            self.ops[e].append(fn)



def emit_quant_shard(p, sb, src, dst, R, C):
    n_r, n_c = R // 128, C // 512
    ev = {"cast": None, "absgn": None, "scaleb": None, "thr": None,
          "bcast": None, "out": None, "store": None, "asum": None}
    for i in range(n_r * n_c):
        r, c = divmod(i, n_c)
        rr = slice(r * 128, (r + 1) * 128)
        cc = slice(c * 512, (c + 1) * 512)
        p.wait("sync", "DV", ev["cast"])
        ev_ld = p.dma("sync", sb.q_w32, src[rr, cc], "Wz")
        p.wait("vector", "Wz", ev_ld)
        p.wait("vector", "AC", ev["absgn"])
        ev["cast"] = p.op("vector", "tensor_copy", sb.q_wb, sb.q_w32, inc="DV")
        p.wait("scalar", "DV", ev["cast"])
        ev["absgn"] = p.op("scalar", "activation", sb.q_aw, sb.q_wb, AF.Abs, inc="AC")
        p.op("scalar", "activation", sb.q_sgn, sb.q_wb, AF.Sign)
        p.wait("tensor", "AC", ev["absgn"])
        ev["asum"] = p.op("tensor", "matmul", sb.pR1[:, :], sb.selb[:, :], sb.q_aw,
                          start=True, stop=True, inc="PE")
        p.wait("scalar", "PE", ev["asum"])
        ev["scaleb"] = p.op("scalar", "activation", sb.q_scaleb[:, :], sb.pR1[:, :], AF.Copy,
                            bias=0.0, scale=1.0 / 64, inc="AC")
        p.wait("vector", "AC", ev["scaleb"])
        p.wait("vector", "PE", ev["bcast"])
        p.op("vector", "tensor_scalar_max", sb.q_scaleb2[:, :], sb.q_scaleb[:, :], 1e-8)
        p.op("vector", "tensor_copy", sb.q_scalef, sb.q_scaleb2[:, :])
        ev["thr"] = p.op("vector", "tensor_scalar_mul", sb.q_thr, sb.q_scalef, THR_C, inc="DV")
        p.wait("tensor", "DV", ev["thr"])
        p.op("tensor", "matmul", sb.pE[:, :], sb.selk2[:, :], sb.q_thr, start=True, stop=True)
        ev["bcast"] = p.op("tensor", "matmul", sb.pF[:, :], sb.selk2[:, :], sb.q_scalef,
                           start=True, stop=True, inc="PE")
        p.wait("vector", "PE", ev["bcast"])
        p.wait("vector", "Gq", ev["store"])
        p.op("vector", "tensor_tensor", sb.q_m, sb.q_aw, sb.pE[:, :], ALU.is_gt)
        p.op("vector", "tensor_mul", sb.q_q, sb.q_m, sb.q_sgn)
        p.op("vector", "tensor_tensor", sb.q_what, sb.q_q, sb.pF[:, :], ALU.mult)
        p.op("vector", "tensor_sub", sb.q_d, sb.q_what, sb.q_wb)
        ev["out"] = p.op("vector", "tensor_add", sb.q_out, sb.q_wb, sb.q_d, inc="DV")
        p.wait("gpsimd", "DV", ev["out"])
        ev["store"] = p.dma("gpsimd", dst[rr, cc], sb.q_out, "Gq")
    return ev["store"]


def emit_cast_shard(p, sb, src, dst, R, C):
    n_r, n_c = R // 128, C // 512
    ev_cast = ev_store = None
    for i in range(n_r * n_c):
        r, c = divmod(i, n_c)
        rr = slice(r * 128, (r + 1) * 128)
        cc = slice(c * 512, (c + 1) * 512)
        p.wait("sync", "DV", ev_cast)
        ev_ld = p.dma("sync", sb.q_w32, src[rr, cc], "Wz")
        p.wait("vector", "Wz", ev_ld)
        p.wait("vector", "Gq", ev_store)
        ev_cast = p.op("vector", "tensor_copy", sb.q_wb, sb.q_w32, inc="DV")
        p.wait("gpsimd", "DV", ev_cast)
        ev_store = p.dma("gpsimd", dst[rr, cc], sb.q_wb, "Gq")
    return ev_store


def emit_prep(p, sb):
    for fn, srcw, dstq, dstg, R, C in (
            (emit_cast_shard, sb.wo_sh, sb.b_wo_sh, sb.WoT, 256, D),
            (emit_cast_shard, sb.wqkv_sh, sb.b_wqkv_sh, sb.WqkvT, 256, 3 * D),
            (emit_quant_shard, sb.wp_sh, sb.q_wp_sh, sb.WpT, 1024, D)):
        ev_last = fn(p, sb, srcw, dstq, R, C)
        p.wait("gpsimd", "Gq", ev_last)
        p.op("gpsimd", "collective_compute", "AllGather", ALU.bypass,
             [list(range(NC))], [dstq.ap()], [dstg.ap()], inc="CC")
        p.gate(engines=("sync", "vector", "scalar", "tensor"))
    for k in range(4):
        ev_last = emit_quant_shard(p, sb, sb.wgu_sh[:, k * 4096:(k + 1) * 4096],
                                   sb.q_wgu_slab[k], 256, 4096)
        p.wait("gpsimd", "Gq", ev_last)
        p.op("gpsimd", "collective_compute", "AllGather", ALU.bypass,
             [list(range(NC))], [sb.q_wgu_slab[k].ap()], [sb.WguT_slab[k].ap()], inc="CC")
        p.gate(engines=("sync", "vector", "scalar", "tensor"))
    return p.ctr["CC"]



def emit_ln_pass(p, sb, x_dram, ln_dram):
    ev_prev_apply = None
    ev_prev_rows = None
    ev_prev_bcast_read = None
    for tt in range(NTT):
        ctt = slice(tt * 512, (tt + 1) * 512)
        if tt == 0:
            p.gate()
        p.wait("sync", "DV", ev_prev_apply)
        ev_ld = p.dma("sync", sb.xbuf.rearrange("p (s f) -> p s f", s=16),
                      x_dram[:, ctt].rearrange("(s p) f -> p s f", p=128), "Wln")
        ev_stat = [None] * 16
        ev_sq = [None] * 16
        for s in range(16):
            sl2 = slice((s % 4) * 512, (s % 4) * 512 + 512)
            if s == 0:
                p.wait("vector", "Wln", ev_ld)
            if s >= 4:
                p.wait("vector", "PE", ev_stat[s - 4])
            ev_c = p.op("vector", "tensor_copy", sb.t_bf[:, sl2], sb.xbuf[:, s * 512:(s + 1) * 512], inc="DV")
            p.wait("scalar", "DV", ev_c)
            if s >= 4:
                p.wait("scalar", "PE", ev_stat[s - 4])
            ev_sq[s] = p.op("scalar", "activation", sb.t_bf2[:, sl2], sb.t_bf[:, sl2], AF.Square, inc="AC")
            p.wait("tensor", "AC", ev_sq[s])
            if s == 0:
                p.wait("tensor", "DV", ev_prev_rows)
            p.op("tensor", "matmul", sb.pR1[0:1, :], sb.ones_col[:, :], sb.t_bf[:, sl2],
                 start=(s == 0), stop=(s == 15))
            ev_stat[s] = p.op("tensor", "matmul", sb.pR2[0:1, :], sb.ones_col[:, :], sb.t_bf2[:, sl2],
                              start=(s == 0), stop=(s == 15), inc="PE")
        p.wait("vector", "PE", ev_stat[15])
        p.op("vector", "tensor_scalar", sb.r_mean[0:1, :], sb.pR1[0:1, :], 1.0 / D, None, ALU.mult)
        ev_prev_rows = p.op("vector", "tensor_scalar", sb.r_msq[0:1, :], sb.pR2[0:1, :], 1.0 / D, None,
                            ALU.mult, inc="DV")
        p.op("vector", "tensor_mul", sb.r_tmp[0:1, :], sb.r_mean[0:1, :], sb.r_mean[0:1, :])
        p.op("vector", "tensor_sub", sb.r_var[0:1, :], sb.r_msq[0:1, :], sb.r_tmp[0:1, :])
        p.op("vector", "tensor_scalar", sb.r_var2[0:1, :], sb.r_var[0:1, :], 1.0, LN_EPS, ALU.mult, ALU.add)
        p.op("vector", "reciprocal", sb.r_rv[0:1, :], sb.r_var2[0:1, :])
        for _ in range(8):
            p.op("vector", "engine_nop")
        ev_rv = p.op("vector", "engine_nop", inc="DV")
        p.wait("scalar", "DV", ev_rv)
        ev_r = p.op("scalar", "activation", sb.r_rstd[0:1, :], sb.r_rv[0:1, :], AF.Sqrt, inc="AC")
        p.wait("vector", "AC", ev_r)
        ev_nmr = p.op("vector", "scalar_tensor_tensor", sb.r_nmr[0:1, :], sb.r_mean[0:1, :], -1.0,
                      sb.r_rstd[0:1, :], ALU.mult, ALU.mult, inc="DV")
        p.wait("tensor", "DV", ev_nmr)
        p.wait("tensor", "DV", ev_prev_bcast_read)
        p.op("tensor", "matmul", sb.pE[:, :], sb.ones_rowf[:, :], sb.r_rstd[0:1, :], start=True, stop=True)
        ev_bc = p.op("tensor", "matmul", sb.pF[:, :], sb.ones_rowf[:, :], sb.r_nmr[0:1, :],
                     start=True, stop=True, inc="PE")
        ev_ap = [None] * 16
        ev_st = [None] * 16
        for s in range(16):
            sl2 = slice((s % 2) * 512, (s % 2) * 512 + 512)
            p.wait("vector", "PE", ev_bc)
            if s >= 2:
                p.wait("vector", "Gl0" if s % 2 == 0 else "Gl1", ev_st[s - 2])
            p.op("vector", "tensor_tensor", sb.t_f32[:, sl2], sb.xbuf[:, s * 512:(s + 1) * 512],
                 sb.pE[:, :], ALU.mult)
            ev_ap[s] = p.op("vector", "tensor_tensor", sb.t_lnout[:, sl2], sb.t_f32[:, sl2],
                            sb.pF[:, :], ALU.add, inc="DV")
            p.wait("gpsimd", "DV", ev_ap[s])
            ev_st[s] = p.dma("gpsimd", ln_dram[s * 128:(s + 1) * 128, ctt], sb.t_lnout[:, sl2],
                             "Gl0" if s % 2 == 0 else "Gl1")
        ev_prev_apply = ev_ap[15]
        ev_prev_bcast_read = ev_ap[15]
        p.wait("gpsimd", "Gl0", ev_st[14])
        p.wait("gpsimd", "Gl1", ev_st[15])
    p.barrier()



def emit_ffn(p, sb, ln_dram, x_in_dram, x_out_dram, dbg=None, cc_wait=None):
    NQ = 32
    if cc_wait:
        p.wait("sync", "CC", cc_wait)
        p.wait("gpsimd", "CC", cc_wait)
    for tt in range(NTT):
        ctt = slice(tt * 512, (tt + 1) * 512)
        p.gate()
        ev_lnall = p.dma("sync", sb.lnbuf.rearrange("p (s f) -> p s f", s=16),
                         ln_dram[:, ctt].rearrange("(s p) f -> p s f", p=128), "Wln")
        ev_xall = p.dma("scalar", sb.xbuf.rearrange("p (s f) -> p s f", s=16),
                        x_in_dram[:, ctt].rearrange("(s p) f -> p s f", p=128), "Xt")

        pair_mm = [None] * 64
        pair_silu = [None] * 64
        pair_h = [None] * 64
        pair_sq = [None] * 64
        pair_ssq = [None] * 64
        psum_g = [sb.pA, sb.pC, sb.pE]
        psum_u = [sb.pB, sb.pD, sb.pF]

        def emit_ssq(pr):
            p.wait("tensor", "AC", pair_sq[pr])
            pair_ssq[pr] = p.op("tensor", "matmul", sb.pR1[0:1, :], sb.ones_col[:, :],
                                sb.h2buf[:, (pr % 4) * 512:(pr % 4) * 512 + 512],
                                start=(pr == 0), stop=(pr == 63), inc="PE")

        for qd in range(NQ):
            buf = qd % 2
            gsl = sb.wstream[:, buf * 4096:buf * 4096 + 4096]
            usl = sb.wstream[:, 8192 + buf * 4096:8192 + buf * 4096 + 4096]
            if qd >= 2:
                p.wait("sync", "PE", pair_mm[2 * (qd - 2) + 1])
            gcol = 256 * qd
            ucol = HF + 256 * qd
            ev_wg = p.dma("sync", gsl.rearrange("p (c o) -> p c o", c=16),
                          sb.WguT_slab[gcol // 4096][:, gcol % 4096:gcol % 4096 + 256]
                          .rearrange("(c p) o -> p c o", p=128), "Wg%d" % buf)
            ev_wu = p.dma("sync", usl.rearrange("p (c o) -> p c o", c=16),
                          sb.WguT_slab[ucol // 4096][:, ucol % 4096:ucol % 4096 + 256]
                          .rearrange("(c p) o -> p c o", p=128), "Wu%d" % buf)
            if qd >= 1:
                emit_ssq(2 * (qd - 1))
                emit_ssq(2 * (qd - 1) + 1)
            p.wait("tensor", "Wg%d" % buf, ev_wg)
            p.wait("tensor", "Wu%d" % buf, ev_wu)
            if qd == 0:
                p.wait("tensor", "Wln", ev_lnall)
            for j in range(2):
                pr = 2 * qd + j
                sl3 = pr % 3
                if pr >= 3:
                    p.wait("tensor", "AC", pair_silu[pr - 3])
                    p.wait("tensor", "DV", pair_h[pr - 3])
                for c in range(16):
                    p.op("tensor", "matmul", psum_g[sl3][:, :],
                         gsl[:, c * 256 + j * 128:c * 256 + j * 128 + 128],
                         sb.lnbuf[:, c * 512:(c + 1) * 512], start=(c == 0), stop=(c == 15))
                for c in range(16):
                    pair_mm[pr] = p.op("tensor", "matmul", psum_u[sl3][:, :],
                                       usl[:, c * 256 + j * 128:c * 256 + j * 128 + 128],
                                       sb.lnbuf[:, c * 512:(c + 1) * 512], start=(c == 0), stop=(c == 15),
                                       inc="PE" if c == 15 else None)
            for j in range(2):
                pr = 2 * qd + j
                sl3 = pr % 3
                p.wait("scalar", "PE", pair_mm[pr])
                if pr >= 3:
                    p.wait("scalar", "DV", pair_h[pr - 3])
                pair_silu[pr] = p.op("scalar", "activation", sb.sgbuf[:, sl3 * 512:sl3 * 512 + 512],
                                     psum_g[sl3][:, :], AF.Silu, inc="AC")
                p.wait("vector", "AC", pair_silu[pr])
                pair_h[pr] = p.op("vector", "tensor_tensor", sb.hbuf[:, pr * 512:(pr + 1) * 512],
                                  sb.sgbuf[:, sl3 * 512:sl3 * 512 + 512], psum_u[sl3][:, :], ALU.mult, inc="DV")
                p.wait("scalar", "DV", pair_h[pr])
                if pr >= 4:
                    p.wait("scalar", "PE", pair_ssq[pr - 4])
                pair_sq[pr] = p.op("scalar", "activation", sb.h2buf[:, (pr % 4) * 512:(pr % 4) * 512 + 512],
                                   sb.hbuf[:, pr * 512:(pr + 1) * 512], AF.Square, inc="AC")
        emit_ssq(62)
        emit_ssq(63)

        p.wait("vector", "PE", pair_ssq[63])
        p.op("vector", "tensor_scalar", sb.r_ms[0:1, :], sb.pR1[0:1, :], 1.0 / HF, RMS_EPS, ALU.mult, ALU.add)
        p.op("vector", "reciprocal", sb.r_rv[0:1, :], sb.r_ms[0:1, :])
        for _ in range(8):
            p.op("vector", "engine_nop")
        ev_rv = p.op("vector", "engine_nop", inc="DV")
        p.wait("scalar", "DV", ev_rv)
        ev_rmsr = p.op("scalar", "activation", sb.r_rstd[0:1, :], sb.r_rv[0:1, :], AF.Sqrt, inc="AC")
        p.wait("tensor", "AC", ev_rmsr)
        p.wait("tensor", "DV", pair_h[62])
        ev_rmsbc = p.op("tensor", "matmul", sb.pE[:, :], sb.ones_rowf[:, :], sb.r_rstd[0:1, :],
                        start=True, stop=True, inc="PE")
        p.wait("vector", "PE", ev_rmsbc)
        p.op("vector", "tensor_copy", sb.rms_bs[:, :], sb.pE[:, :])
        if dbg is not None and tt == 0:
            p.gate()
            p.dma("gpsimd", dbg["h"].ap(), sb.hbuf[:, :], "Gd")
            p.dma("gpsimd", dbg["ln"].ap(), sb.lnbuf[:, :], "Gd")
            p.dma("gpsimd", dbg["rms"].ap(), sb.rms_bs[:, :], "Gd")
            p.wait("gpsimd", "Gd", p.ctr["Gd"])
            p.gate()

        pbank = [sb.pA, sb.pB, sb.pC, sb.pD]
        slab_pe = {}
        ev_evac = {}
        ev_store = {}
        p.wait("sync", "PE", pair_mm[63])
        idx = 0
        for s4 in range(4):
            for hc in range(64):
                wbuf = idx % 8
                slab = sb.wstream[:, wbuf * 512:wbuf * 512 + 512]
                if idx >= 8:
                    p.wait("sync", "PE", slab_pe[idx - 8])
                ev_w = p.dma("sync", slab, sb.WpT[hc * 128:(hc + 1) * 128, s4 * 512:s4 * 512 + 512],
                             "Wp%d" % wbuf)
                p.wait("tensor", "Wp%d" % wbuf, ev_w)
                if hc == 0 and s4 > 0:
                    p.wait("tensor", "DV", ev_evac[s4 * 4 - 1])
                if hc == 0 and s4 == 0:
                    p.wait("tensor", "AC", pair_silu[63])
                    p.wait("tensor", "DV", pair_h[63])
                for sl in range(4):
                    slab_pe[idx] = p.op("tensor", "matmul", pbank[sl][:, :],
                                        slab[:, sl * 128:sl * 128 + 128],
                                        sb.hbuf[:, hc * 512:(hc + 1) * 512],
                                        start=(hc == 0), stop=(hc == 63),
                                        inc="PE" if sl == 3 else None)
                idx += 1
            for sl in range(4):
                s = s4 * 4 + sl
                sl2 = slice((s % 2) * 512, (s % 2) * 512 + 512)
                p.wait("vector", "PE", slab_pe[idx - 1])
                if s == 0:
                    p.wait("vector", "Xt", ev_xall)
                if s >= 2:
                    p.wait("vector", "Go%d" % (s % 2), ev_store[s - 2])
                p.op("vector", "tensor_tensor", sb.t_f32[:, sl2], pbank[sl][:, :], sb.rms_bs[:, :], ALU.mult)
                ev_evac[s] = p.op("vector", "scalar_tensor_tensor", sb.xo[:, sl2], sb.t_f32[:, sl2], 0.5,
                                  sb.xbuf[:, s * 512:(s + 1) * 512], ALU.mult, ALU.add, inc="DV")
                p.wait("gpsimd", "DV", ev_evac[s])
                ev_store[s] = p.dma("gpsimd", x_out_dram[s * 128:(s + 1) * 128, ctt], sb.xo[:, sl2],
                                    "Go%d" % (s % 2))
        if dbg is not None and tt == 0:
            p.gate()
            p.dma("gpsimd", dbg["h2"].ap(), sb.hbuf[:, :], "Gd")
            p.wait("gpsimd", "Gd", p.ctr["Gd"])
            p.gate()
        p.wait("gpsimd", "Go0", ev_store[14])
        p.wait("gpsimd", "Go1", ev_store[15])
    p.barrier()



def emit_attn(p, sb):
    qk_psums = [sb.pA, sb.pB, sb.pC, sb.pD]
    for hg in range(NH // 2):
        p.gate()
        base = hg * 256
        p.dma("sync", sb.wstream[:, 0:4096].rearrange("p (c o) -> p c o", c=16),
              sb.WqkvT[:, base:base + 256].rearrange("(c p) o -> p c o", p=128), "Wq")
        p.dma("sync", sb.wstream[:, 4096:8192].rearrange("p (c o) -> p c o", c=16),
              sb.WqkvT[:, D + base:D + base + 256].rearrange("(c p) o -> p c o", p=128), "Wq")
        ev_wv = p.dma("sync", sb.wstream[:, 8192:12288].rearrange("p (c o) -> p c o", c=16),
                      sb.WqkvT[:, 2 * D + base:2 * D + base + 256].rearrange("(c p) o -> p c o", p=128), "Wq")
        p.wait("tensor", "Wq", ev_wv)

        ev_chunk = {}
        ev_qkevac = None
        for tg in range(4):
            ev_mm = None
            for c in range(16):
                i = tg * 16 + c
                lbuf = i % 4
                lsl = sb.lnsl[:, lbuf * 512:lbuf * 512 + 512]
                if i >= 4:
                    p.wait("sync", "PE", ev_chunk[i - 4])
                ev_ln = p.dma("sync", lsl, sb.ln2T[c * 128:(c + 1) * 128, tg * 512:(tg + 1) * 512],
                              "Wl%d" % lbuf)
                p.wait("tensor", "Wl%d" % lbuf, ev_ln)
                if c == 0 and tg > 0:
                    p.wait("tensor", "DV", ev_qkevac)
                for sl in range(4):
                    wofs = [0, 4096, 0, 4096][sl] + c * 256 + (sl // 2) * 128
                    p.op("tensor", "matmul", qk_psums[sl][:, :], sb.wstream[:, wofs:wofs + 128],
                         lsl, start=(c == 0), stop=(c == 15))
                for ts in range(4):
                    vp = [sb.pE, sb.pF][ts // 2][:, (ts % 2) * 256:(ts % 2) * 256 + 256]
                    ev_mm = p.op("tensor", "matmul", vp,
                                 lsl[:, ts * 128:ts * 128 + 128],
                                 sb.wstream[:, 8192 + c * 256:8192 + c * 256 + 256],
                                 start=(c == 0 and ts % 2 == 0), stop=(c == 15),
                                 inc="PE" if ts == 3 else None)
                ev_chunk[i] = ev_mm
            p.wait("vector", "PE", ev_mm)
            for sl in range(4):
                p.op("vector", "tensor_copy",
                     sb.lnbuf[:, sl * 2048 + tg * 512:sl * 2048 + (tg + 1) * 512], qk_psums[sl][:, :])
            ev_qkevac = None
            for ts in range(4):
                tsub = tg * 4 + ts
                pp = [sb.pE, sb.pF][ts // 2]
                for j in range(2):
                    ev_qkevac = p.op("vector", "tensor_copy",
                                     sb.vbuf[:, j * 2048 + tsub * 128:j * 2048 + (tsub + 1) * 128],
                                     pp[:, (ts % 2) * 256 + j * 128:(ts % 2) * 256 + j * 128 + 128],
                                     inc="DV" if (ts == 3 and j == 1) else None)
        p.gate()

        for j in range(2):
            p.gate(engines=("scalar", "vector", "tensor"))
            h = 2 * hg + j
            Q = sb.lnbuf[:, (2 * j) * 2048:(2 * j) * 2048 + 2048]
            K = sb.lnbuf[:, (2 * j + 1) * 2048:(2 * j + 1) * 2048 + 2048]
            V = sb.vbuf[:, j * 2048:(j + 1) * 2048]
            for src, dst in ((Q, sb.r_qn2), (K, sb.r_kn2)):
                for i4 in range(4):
                    ssl = slice((i4 % 2) * 512, (i4 % 2) * 512 + 512)
                    if True:
                        pass
                    ev_sq = p.op("scalar", "activation", sb.pe_scr[:, ssl], src[:, i4 * 512:(i4 + 1) * 512],
                                 AF.Square, inc="AC")
                    p.wait("tensor", "AC", ev_sq)
                    ev_n = p.op("tensor", "matmul", sb.pR2[0:1, :], sb.ones_col[:, :], sb.pe_scr[:, ssl],
                                start=True, stop=True, inc="PE")
                    p.wait("vector", "PE", ev_n)
                    ev_cp = p.op("vector", "tensor_copy", dst[0:1, i4 * 512:(i4 + 1) * 512],
                                 sb.pR2[0:1, :], inc="DV")
                    p.wait("scalar", "DV", ev_cp)
            p.op("vector", "tensor_reduce", sb.r_sc1[0:1, 0:1], sb.r_qn2[0:1, :], AXX, ALU.max)
            p.op("vector", "tensor_reduce", sb.r_sc2[0:1, 0:1], sb.r_kn2[0:1, :], AXX, ALU.max)
            for _ in range(8):
                p.op("vector", "engine_nop")
            ev_red = p.op("vector", "engine_nop", inc="DV")
            p.wait("gpsimd", "DV", ev_red)
            p.op("gpsimd", "tensor_add", sb.r_sc3[0:1, 0:1], sb.r_sc1[0:1, 0:1], sb.r_sc2[0:1, 0:1])
            ev_nb = p.op("gpsimd", "tensor_scalar_mul", sb.r_sc3[0:1, 0:1], sb.r_sc3[0:1, 0:1],
                         -SM_SCALE / 2, inc="GP")
            p.wait("tensor", "GP", ev_nb)
            ev_nbc = p.op("tensor", "matmul", sb.pC[:, 0:1], sb.ones_rowf[:, :], sb.r_sc3[0:1, 0:1],
                          start=True, stop=True, inc="PE")
            p.wait("vector", "PE", ev_nbc)
            ev_nbcol = p.op("vector", "tensor_copy", sb.nbcol[:, :], sb.pC[:, 0:1], inc="DV")
            p.wait("scalar", "DV", ev_nbcol)
            p.wait("tensor", "DV", ev_nbcol)

            ev_exp_prev_qg = None
            ev_at_prev = None
            for qg in range(4):
                nvis = 4 * qg + 4
                ev_exp = [None] * 64
                ev_pt = [None] * 64
                ev_lmm = [None] * 64
                banks = [sb.pA, sb.pB, sb.pD, sb.pF, sb.pE]

                def emit_pv(kc, last):
                    sem, val = ev_pt[kc]
                    p.wait("tensor", sem, val)
                    p.op("tensor", "matmul", sb.pC[:, :], V[:, kc * 128:(kc + 1) * 128],
                         sb.ptbuf[:, (kc % 4) * 512:(kc % 4) * 512 + 512],
                         start=(kc == 0), stop=last)
                    ev_lmm[kc] = p.op("tensor", "matmul", sb.pR1[0:1, :], sb.ones_col[:, :],
                                      sb.ptbuf[:, (kc % 4) * 512:(kc % 4) * 512 + 512],
                                      start=(kc == 0), stop=last, inc="PE")

                for kc in range(nvis):
                    pb = (kc + qg) % 5
                    psc = banks[pb]
                    if kc >= 5:
                        p.wait("tensor", "AC", ev_exp[kc - 5])
                    elif ev_exp_prev_qg:
                        p.wait("tensor", "AC", ev_exp_prev_qg)
                    ev_sc = p.op("tensor", "matmul", psc[:, :], K[:, kc * 128:(kc + 1) * 128],
                                 Q[:, qg * 512:(qg + 1) * 512], start=True, stop=True, inc="PE")
                    p.wait("scalar", "PE", ev_sc)
                    diag = kc - 4 * qg
                    sl4 = kc % 4
                    if diag >= 0:
                        if kc >= 4:
                            p.wait("scalar", ev_pt[kc - 4][0], ev_pt[kc - 4][1])
                        ev_e = p.op("scalar", "activation", sb.pe_scr[:, sl4 * 512:sl4 * 512 + 512],
                                    psc[:, :], AF.Exp, bias=sb.nbcol[:, :], scale=SM_SCALE, inc="AC")
                        ev_exp[kc] = ev_e
                        p.wait("vector", "AC", ev_e)
                        if kc >= 4:
                            p.wait("vector", "PE", ev_lmm[kc - 4])
                        ev_pt[kc] = ("DV", p.op("vector", "tensor_tensor",
                                                sb.ptbuf[:, sl4 * 512:sl4 * 512 + 512],
                                                sb.pe_scr[:, sl4 * 512:sl4 * 512 + 512],
                                                sb.maskbuf[:, diag * 512:(diag + 1) * 512],
                                                ALU.mult, inc="DV"))
                    else:
                        if kc >= 4:
                            p.wait("scalar", "PE", ev_lmm[kc - 4])
                        ev_e = p.op("scalar", "activation", sb.ptbuf[:, sl4 * 512:sl4 * 512 + 512],
                                    psc[:, :], AF.Exp, bias=sb.nbcol[:, :], scale=SM_SCALE, inc="AC")
                        ev_exp[kc] = ev_e
                        ev_pt[kc] = ("AC", ev_e)
                    if kc >= 3:
                        if kc == 3 and ev_at_prev is not None:
                            p.wait("tensor", "DV", ev_at_prev)
                        emit_pv(kc - 3, last=False)
                for tail in (3, 2, 1):
                    emit_pv(nvis - tail, last=(tail == 1))
                ev_l = ev_lmm[nvis - 1]
                ev_exp_prev_qg = ev_exp[nvis - 1]
                p.wait("vector", "PE", ev_l)
                p.op("vector", "tensor_copy", sb.r_l[0:1, :], sb.pR1[0:1, :])
                p.op("vector", "reciprocal", sb.r_rl[0:1, :], sb.r_l[0:1, :])
                for _ in range(8):
                    p.op("vector", "engine_nop")
                ev_rl = p.op("vector", "engine_nop", inc="DV")
                p.wait("tensor", "DV", ev_rl)
                rlbank = banks[qg % 5]
                ev_rlbc = p.op("tensor", "matmul", rlbank[:, :], sb.ones_rowf[:, :], sb.r_rl[0:1, :],
                               start=True, stop=True, inc="PE")
                p.wait("vector", "PE", ev_rlbc)
                p.op("vector", "tensor_copy", sb.rls[:, :], rlbank[:, :])
                ev_at = p.op("vector", "tensor_tensor",
                             sb.hbuf[:, h * 2048 + qg * 512:h * 2048 + (qg + 1) * 512],
                             sb.pC[:, :], sb.rls[:, :], ALU.mult, inc="DV")
                ev_at_prev = ev_at

    pbank = [sb.pA, sb.pB, sb.pC, sb.pD]
    for tt in range(NTT):
        ctt = slice(tt * 512, (tt + 1) * 512)
        for s4 in range(4):
            p.gate()
            ev_x1 = p.dma("scalar", sb.xbuf[:, 0:2048].rearrange("p (s f) -> p s f", s=4),
                          sb.x1T[s4 * 512:(s4 + 1) * 512, ctt].rearrange("(s p) f -> p s f", p=128),
                          "Xt")
            slab_pe = {}
            for ac in range(16):
                wbuf = ac % 8
                slab = sb.wstream[:, wbuf * 512:wbuf * 512 + 512]
                if ac >= 8:
                    p.wait("sync", "PE", slab_pe[ac - 8])
                ev_w = p.dma("sync", slab, sb.WoT[ac * 128:(ac + 1) * 128, s4 * 512:s4 * 512 + 512],
                             "Wp%d" % wbuf)
                p.wait("tensor", "Wp%d" % wbuf, ev_w)
                for sl in range(4):
                    slab_pe[ac] = p.op("tensor", "matmul", pbank[sl][:, :],
                                       slab[:, sl * 128:sl * 128 + 128],
                                       sb.hbuf[:, ac * 2048 + tt * 512:ac * 2048 + (tt + 1) * 512],
                                       start=(ac == 0), stop=(ac == 15),
                                       inc="PE" if sl == 3 else None)
            ev_store = {}
            for sl in range(4):
                s = s4 * 4 + sl
                sl2 = slice((sl % 2) * 512, (sl % 2) * 512 + 512)
                p.wait("vector", "PE", slab_pe[15])
                if sl == 0:
                    p.wait("vector", "Xt", ev_x1)
                if sl >= 2:
                    p.wait("vector", "Go%d" % (sl % 2), ev_store[sl - 2])
                ev_e = p.op("vector", "tensor_tensor", sb.xo[:, sl2], pbank[sl][:, :],
                            sb.xbuf[:, sl * 512:(sl + 1) * 512], ALU.add, inc="DV")
                p.wait("gpsimd", "DV", ev_e)
                ev_store[sl] = p.dma("gpsimd", sb.x2T[s * 128:(s + 1) * 128, ctt], sb.xo[:, sl2],
                                     "Go%d" % (sl % 2))
            p.wait("gpsimd", "Go0", ev_store[2])
            p.wait("gpsimd", "Go1", ev_store[3])
    p.barrier()



def build_nc(debug_outputs=()):
    nc = bass.Bass("TRN2", target_bir_lowering=False, detect_race_conditions=False)

    class Ctx:
        pass

    sb = Ctx()
    sb.x0T = nc.dram_tensor("x0T", [D, S], F32, kind="ExternalInput")
    sb.wgu_sh = nc.dram_tensor("wgu_sh", [256, 2 * HF], F32, kind="ExternalInput")
    sb.wp_sh = nc.dram_tensor("wp_sh", [1024, D], F32, kind="ExternalInput")
    sb.wqkv_sh = nc.dram_tensor("wqkv_sh", [256, 3 * D], F32, kind="ExternalInput")
    sb.wo_sh = nc.dram_tensor("wo_sh", [256, D], F32, kind="ExternalInput")
    sb.yT = nc.dram_tensor("yT", [D, S], F32, kind="ExternalOutput")

    def internal(name, shape, dt):
        kind = "ExternalOutput" if name in debug_outputs else "Internal"
        return nc.dram_tensor(name, shape, dt, kind=kind)

    sb.q_wgu_slab = [internal(f"q_wgu_s{k}", [256, 4096], BF16) for k in range(4)]
    sb.WguT_slab = [internal(f"WguT_s{k}", [D, 4096], BF16) for k in range(4)]
    sb.q_wp_sh = internal("q_wp_sh", [1024, D], BF16)
    sb.b_wqkv_sh = internal("b_wqkv_sh", [256, 3 * D], BF16)
    sb.b_wo_sh = internal("b_wo_sh", [256, D], BF16)

    sb.WpT = internal("WpT", [HF, D], BF16)
    sb.WqkvT = internal("WqkvT", [D, 3 * D], BF16)
    sb.WoT = internal("WoT", [D, D], BF16)
    sb.dbg_h = internal("dbg_h", [128, 32768], BF16)
    sb.dbg_h2 = internal("dbg_h2", [128, 32768], BF16)
    sb.dbg_ln = internal("dbg_ln", [128, 8192], BF16)
    sb.dbg_rms = internal("dbg_rms", [128, 512], F32)
    sb.x1T = internal("x1T", [D, S], F32)
    sb.x2T = internal("x2T", [D, S], F32)
    sb.ln1T = internal("ln1T", [D, S], BF16)
    sb.ln2T = internal("ln2T", [D, S], BF16)
    sb.ln3T = internal("ln3T", [D, S], BF16)

    selb_np = np.zeros((128, 2), dtype=BF)
    selb_np[:64, 0] = 1
    selb_np[64:, 1] = 1
    selk2_np = np.zeros((2, 128), dtype=np.float32)
    selk2_np[0, :64] = 1
    selk2_np[1, 64:] = 1
    ones_col_np = np.ones((128, 1), dtype=BF)
    ones_rowf_np = np.ones((1, 128), dtype=np.float32)
    masks_np = np.zeros((4 * 128, 512), dtype=BF)
    for jj in range(4):
        for pp in range(128):
            masks_np[jj * 128 + pp, 128 * jj + pp:] = 1
    C_SELB = nc.inline_tensor(selb_np, "c_selb")
    C_SELK2 = nc.inline_tensor(selk2_np, "c_selk2")
    C_ONESC = nc.inline_tensor(ones_col_np, "c_onesc")
    C_ONESRF = nc.inline_tensor(ones_rowf_np, "c_onesrf")
    C_MASKS = nc.inline_tensor(masks_np, "c_masks")

    with ExitStack() as stack:
        def sbuf(name, shape, dt):
            return stack.enter_context(nc.sbuf_tensor(name, shape, dt))

        def psum(name, shape, dt):
            return stack.enter_context(nc.psum_tensor(name, shape, dt))

        sb.lnbuf = sbuf("lnbuf", [128, 8192], BF16)
        sb.hbuf = sbuf("hbuf", [128, 32768], BF16)
        sb.xbuf = sbuf("xbuf", [128, 8192], F32)
        sb.wstream = sbuf("wstream", [128, 16384], BF16)
        sb.vbuf = sbuf("vbuf", [128, 4096], BF16)
        sb.sgbuf = sbuf("sgbuf", [128, 1536], F32)
        sb.h2buf = None
        sb.t_bf = sbuf("t_bf", [128, 2048], BF16)
        sb.t_bf2 = sbuf("t_bf2", [128, 2048], BF16)
        sb.t_f32 = sbuf("t_f32", [128, 1024], F32)
        sb.t_lnout = None
        sb.xo = sbuf("xo", [128, 1024], F32)
        sb.rms_bs = sbuf("rms_bs", [128, 512], F32)
        sb.rls = sbuf("rls", [128, 512], F32)
        sb.ptbuf = sbuf("ptbuf", [128, 2048], BF16)
        sb.pe_scr = sbuf("pe_scr", [128, 2048], BF16)
        sb.h2buf = sb.ptbuf
        sb.t_lnout = sb.pe_scr[:, 1024:2048]
        sb.lnsl = sbuf("lnsl", [128, 2048], BF16)
        sb.maskbuf = sbuf("maskbuf", [128, 2048], BF16)
        sb.nbcol = sbuf("nbcol", [128, 1], F32)
        sb.q_scaleb = sbuf("q_scaleb", [2, 512], BF16)
        sb.q_scaleb2 = sbuf("q_scaleb2", [2, 512], BF16)
        sb.q_scalef = sb.sgbuf[0:2, 512:1024]
        sb.q_thr = sb.sgbuf[0:2, 0:512]
        sb.r_mean = sbuf("r_mean", [1, 512], F32)
        sb.r_msq = sbuf("r_msq", [1, 512], F32)
        sb.r_rv = sbuf("r_rv", [1, 512], F32)
        sb.r_rstd = sbuf("r_rstd", [1, 512], F32)
        sb.r_tmp = sb.sgbuf[0:1, 1024:1536]
        sb.r_var = sb.r_rstd
        sb.r_nmr = sb.r_rv
        sb.r_var2 = sb.r_tmp
        sb.r_ms = sb.r_var
        sb.r_l = sb.r_mean
        sb.r_rl = sb.r_msq
        sb.q_w32 = sb.t_f32[:, 0:512]
        sb.q_wb = sb.t_bf[:, 0:512]
        sb.q_d = sb.t_bf[:, 512:1024]
        sb.q_aw = sb.t_bf2[:, 0:512]
        sb.q_sgn = sb.t_lnout[:, 0:512]
        sb.q_m = sb.ptbuf[:, 0:512]
        sb.q_out = sb.ptbuf[:, 512:1024]
        sb.q_q = sb.pe_scr[:, 0:512]
        sb.q_what = sb.lnsl[:, 0:512]
        sb.r_qn2 = sb.xbuf[0:1, 0:2048]
        sb.r_kn2 = sb.xbuf[0:1, 2048:4096]
        sb.r_sc1 = sbuf("r_sc1", [1, 1], F32)
        sb.r_sc2 = sbuf("r_sc2", [1, 1], F32)
        sb.r_sc3 = sbuf("r_sc3", [1, 1], F32)
        sb.selb = sbuf("selb", [128, 2], BF16)
        sb.selk2 = sbuf("selk2", [2, 128], F32)
        sb.ones_col = sbuf("ones_col", [128, 1], BF16)
        sb.ones_rowf = sbuf("ones_rowf", [1, 128], F32)
        sb.pA = psum("pA", [128, 512], F32)
        sb.pB = psum("pB", [128, 512], F32)
        sb.pC = psum("pC", [128, 512], F32)
        sb.pD = psum("pD", [128, 512], F32)
        sb.pE = psum("pE", [128, 512], F32)
        sb.pF = psum("pF", [128, 512], F32)
        sb.pR1 = psum("pR1", [2, 512], F32)
        sb.pR2 = psum("pR2", [2, 512], F32)

        sems = {n: stack.enter_context(nc.semaphore("s" + n)) for n in SEMNAMES}
        p = P(sems)

        for dst, src in ((sb.selb, C_SELB), (sb.selk2, C_SELK2),
                         (sb.ones_col, C_ONESC), (sb.ones_rowf, C_ONESRF)):
            p.dma("sync", dst[:, :], src[:, :], "Wc")
        for jj in range(4):
            p.dma("sync", sb.maskbuf[:, jj * 512:(jj + 1) * 512], C_MASKS[jj * 128:(jj + 1) * 128, :], "Wc")
        p.gate()

        cc_target = emit_prep(p, sb)
        emit_ln_pass(p, sb, sb.x0T, sb.ln1T)
        dbg = ({"h": sb.dbg_h, "h2": sb.dbg_h2, "ln": sb.dbg_ln, "rms": sb.dbg_rms}
               if "dbg_h" in debug_outputs else None)
        emit_ffn(p, sb, sb.ln1T, sb.x0T, sb.x1T, dbg=dbg, cc_wait=cc_target)
        emit_ln_pass(p, sb, sb.x1T, sb.ln2T)
        emit_attn(p, sb)
        emit_ln_pass(p, sb, sb.x2T, sb.ln3T)
        emit_ffn(p, sb, sb.ln3T, sb.x2T, sb.yT)

        with nc.Block() as block:
            for ename in ENGINES:
                def make_body(ename):
                    def body(eng):
                        for f in p.ops[ename]:
                            f(eng)
                    return body
                getattr(block, ename)(make_body(ename))

    return nc


def kernel(L0_attn_hidden_states, L0_attn_w_qkv, L0_attn_w_o, L0_ffn_w_gate_up, L0_ffn_w_proj):
    hs = np.asarray(L0_attn_hidden_states, dtype=np.float32)
    w_qkv = np.asarray(L0_attn_w_qkv, dtype=np.float32)
    w_o = np.asarray(L0_attn_w_o, dtype=np.float32)
    w_gu = np.asarray(L0_ffn_w_gate_up, dtype=np.float32)
    w_p = np.asarray(L0_ffn_w_proj, dtype=np.float32)

    nc = build_nc()
    in_maps = []
    for c in range(NC):
        in_maps.append({
            "x0T": np.ascontiguousarray(hs[c].T),
            "wgu_sh": np.ascontiguousarray(w_gu[:, c * 256:(c + 1) * 256].T),
            "wp_sh": np.ascontiguousarray(w_p[:, c * 1024:(c + 1) * 1024].T),
            "wqkv_sh": np.ascontiguousarray(w_qkv[:, c * 256:(c + 1) * 256].T),
            "wo_sh": np.ascontiguousarray(w_o[:, c * 256:(c + 1) * 256].T),
        })
    res = run_bass_kernel_spmd(nc, in_maps, core_ids=list(range(NC)))
    return np.stack([np.ascontiguousarray(r["yT"].T) for r in res.results])


# version 08
# speedup vs baseline: 1.0025x; 1.0025x over previous
import math
import numpy as np
import ml_dtypes
from contextlib import ExitStack

import concourse.bass as bass
import concourse.mybir as mybir
from concourse.bass_utils import run_bass_kernel_spmd

F32 = mybir.dt.float32
BF16 = mybir.dt.bfloat16
AF = mybir.ActivationFunctionType
ALU = mybir.AluOpType
AXX = mybir.AxisListType.X

BF = ml_dtypes.bfloat16

D = 2048
S = 2048
NH = 16
HD = 128
HF = 8192
NC = 8
TT = 512
NTT = S // TT
SM_SCALE = 1.0 / math.sqrt(HD)
THR_C = 0.5 * (1.0 + 2.0 ** -9)
LN_EPS = 1e-5
RMS_EPS = float(np.finfo(np.float32).eps)

ENGINES = ("sync", "vector", "scalar", "tensor", "gpsimd")
SEMNAMES = (["Wg0", "Wg1", "Wu0", "Wu1"] + [f"Wp{i}" for i in range(8)]
            + ["Wl0", "Wl1", "Wl2", "Wl3", "Wq", "Wln", "Wz", "Wc", "Xt",
               "Go0", "Go1", "Gl0", "Gl1", "Gq", "Gd",
               "PE", "DV", "AC", "GP", "CC", "B"])
DMA_SEMS = tuple(n for n in SEMNAMES if n[0] in "WXG" and n != "GP")


class P:

    def __init__(self, sems):
        self.ops = {e: [] for e in ENGINES}
        self.sems = sems
        self.ctr = {name: 0 for name in sems}

    def op(self, engine, method, *args, inc=None, inc_n=1, **kwargs):
        val = None
        if inc is not None:
            self.ctr[inc] += inc_n
            val = self.ctr[inc]
            sem = self.sems[inc]

            def fn(eng, method=method, args=args, kwargs=kwargs, sem=sem, n=inc_n):
                getattr(eng, method)(*args, **kwargs).then_inc(sem, n)
        else:
            def fn(eng, method=method, args=args, kwargs=kwargs):
                getattr(eng, method)(*args, **kwargs)
        self.ops[engine].append(fn)
        return val

    def dma(self, engine, out, in_, sem):
        return self.op(engine, "dma_start", inc=sem, inc_n=16, out=out, in_=in_)

    def wait(self, engine, semname, val):
        if not val:
            return
        sem = self.sems[semname]

        def fn(eng, sem=sem, val=val):
            eng.wait_ge(sem, val)
        self.ops[engine].append(fn)

    def gate(self, engines=ENGINES, sems=None):
        if sems is None:
            sems = [n for n in SEMNAMES if n not in ("B", "CC")]
        snap = {k: self.ctr[k] for k in sems}
        own = {"sync": None, "vector": "DV", "scalar": "AC", "tensor": "PE", "gpsimd": "GP"}
        for e in engines:
            for k, v in snap.items():
                if v and own.get(e) != k:
                    self.wait(e, k, v)

    def barrier(self):
        self.ctr["B"] += len(ENGINES)
        target = self.ctr["B"]
        sem = self.sems["B"]
        for e in ENGINES:
            def fn(eng, sem=sem, target=target):
                eng.drain()
                eng.nop().then_inc(sem, 1)
                eng.wait_ge(sem, target)
            self.ops[e].append(fn)



def emit_quant_shard(p, sb, src, dst, R, C):
    n_r, n_c = R // 128, C // 512
    ev = {"cast": None, "absgn": None, "scaleb": None, "thr": None,
          "bcast": None, "out": None, "store": None, "asum": None}
    for i in range(n_r * n_c):
        r, c = divmod(i, n_c)
        rr = slice(r * 128, (r + 1) * 128)
        cc = slice(c * 512, (c + 1) * 512)
        p.wait("sync", "DV", ev["cast"])
        ev_ld = p.dma("sync", sb.q_w32, src[rr, cc], "Wz")
        p.wait("vector", "Wz", ev_ld)
        p.wait("vector", "AC", ev["absgn"])
        ev["cast"] = p.op("vector", "tensor_copy", sb.q_wb, sb.q_w32, inc="DV")
        p.wait("scalar", "DV", ev["cast"])
        ev["absgn"] = p.op("scalar", "activation", sb.q_aw, sb.q_wb, AF.Abs, inc="AC")
        p.op("scalar", "activation", sb.q_sgn, sb.q_wb, AF.Sign)
        p.wait("tensor", "AC", ev["absgn"])
        ev["asum"] = p.op("tensor", "matmul", sb.pR1[:, :], sb.selb[:, :], sb.q_aw,
                          start=True, stop=True, inc="PE")
        p.wait("scalar", "PE", ev["asum"])
        ev["scaleb"] = p.op("scalar", "activation", sb.q_scaleb[:, :], sb.pR1[:, :], AF.Copy,
                            bias=0.0, scale=1.0 / 64, inc="AC")
        p.wait("vector", "AC", ev["scaleb"])
        p.wait("vector", "PE", ev["bcast"])
        p.op("vector", "tensor_scalar_max", sb.q_scaleb2[:, :], sb.q_scaleb[:, :], 1e-8)
        p.op("vector", "tensor_copy", sb.q_scalef, sb.q_scaleb2[:, :])
        ev["thr"] = p.op("vector", "tensor_scalar_mul", sb.q_thr, sb.q_scalef, THR_C, inc="DV")
        p.wait("tensor", "DV", ev["thr"])
        p.op("tensor", "matmul", sb.pE[:, :], sb.selk2[:, :], sb.q_thr, start=True, stop=True)
        ev["bcast"] = p.op("tensor", "matmul", sb.pF[:, :], sb.selk2[:, :], sb.q_scalef,
                           start=True, stop=True, inc="PE")
        p.wait("vector", "PE", ev["bcast"])
        p.wait("vector", "Gq", ev["store"])
        p.op("vector", "tensor_tensor", sb.q_m, sb.q_aw, sb.pE[:, :], ALU.is_gt)
        p.op("vector", "tensor_mul", sb.q_q, sb.q_m, sb.q_sgn)
        p.op("vector", "tensor_tensor", sb.q_what, sb.q_q, sb.pF[:, :], ALU.mult)
        p.op("vector", "tensor_sub", sb.q_d, sb.q_what, sb.q_wb)
        ev["out"] = p.op("vector", "tensor_add", sb.q_out, sb.q_wb, sb.q_d, inc="DV")
        p.wait("gpsimd", "DV", ev["out"])
        ev["store"] = p.dma("gpsimd", dst[rr, cc], sb.q_out, "Gq")
    return ev["store"]


def emit_cast_shard(p, sb, src, dst, R, C):
    n_r, n_c = R // 128, C // 512
    ev_cast = ev_store = None
    for i in range(n_r * n_c):
        r, c = divmod(i, n_c)
        rr = slice(r * 128, (r + 1) * 128)
        cc = slice(c * 512, (c + 1) * 512)
        p.wait("sync", "DV", ev_cast)
        ev_ld = p.dma("sync", sb.q_w32, src[rr, cc], "Wz")
        p.wait("vector", "Wz", ev_ld)
        p.wait("vector", "Gq", ev_store)
        ev_cast = p.op("vector", "tensor_copy", sb.q_wb, sb.q_w32, inc="DV")
        p.wait("gpsimd", "DV", ev_cast)
        ev_store = p.dma("gpsimd", dst[rr, cc], sb.q_wb, "Gq")
    return ev_store


def emit_prep(p, sb):
    for fn, srcw, dstq, dstg, R, C in (
            (emit_cast_shard, sb.wo_sh, sb.b_wo_sh, sb.WoT, 256, D),
            (emit_cast_shard, sb.wqkv_sh, sb.b_wqkv_sh, sb.WqkvT, 256, 3 * D),
            (emit_quant_shard, sb.wp_sh, sb.q_wp_sh, sb.WpT, 1024, D)):
        ev_last = fn(p, sb, srcw, dstq, R, C)
        p.wait("gpsimd", "Gq", ev_last)
        p.op("gpsimd", "collective_compute", "AllGather", ALU.bypass,
             [list(range(NC))], [dstq.ap()], [dstg.ap()], inc="CC")
        p.gate(engines=("sync", "vector", "scalar", "tensor"))
    for k in range(4):
        ev_last = emit_quant_shard(p, sb, sb.wgu_sh[:, k * 4096:(k + 1) * 4096],
                                   sb.q_wgu_slab[k], 256, 4096)
        p.wait("gpsimd", "Gq", ev_last)
        p.op("gpsimd", "collective_compute", "AllGather", ALU.bypass,
             [list(range(NC))], [sb.q_wgu_slab[k].ap()], [sb.WguT_slab[k].ap()], inc="CC")
        p.gate(engines=("sync", "vector", "scalar", "tensor"))
    return p.ctr["CC"]



def emit_ln_pass(p, sb, x_dram, ln_dram):
    ev_prev_apply = None
    ev_prev_rows = None
    ev_prev_bcast_read = None
    for tt in range(NTT):
        ctt = slice(tt * 512, (tt + 1) * 512)
        if tt == 0:
            p.gate()
        p.wait("sync", "DV", ev_prev_apply)
        ev_ld = p.dma("sync", sb.xbuf.rearrange("p (s f) -> p s f", s=16),
                      x_dram[:, ctt].rearrange("(s p) f -> p s f", p=128), "Wln")
        ev_stat = [None] * 16
        ev_sq = [None] * 16
        for s in range(16):
            sl2 = slice((s % 4) * 512, (s % 4) * 512 + 512)
            if s == 0:
                p.wait("vector", "Wln", ev_ld)
            if s >= 4:
                p.wait("vector", "PE", ev_stat[s - 4])
            ev_c = p.op("vector", "tensor_copy", sb.t_bf[:, sl2], sb.xbuf[:, s * 512:(s + 1) * 512], inc="DV")
            p.wait("scalar", "DV", ev_c)
            if s >= 4:
                p.wait("scalar", "PE", ev_stat[s - 4])
            ev_sq[s] = p.op("scalar", "activation", sb.t_bf2[:, sl2], sb.t_bf[:, sl2], AF.Square, inc="AC")
            p.wait("tensor", "AC", ev_sq[s])
            if s == 0:
                p.wait("tensor", "DV", ev_prev_rows)
            p.op("tensor", "matmul", sb.pR1[0:1, :], sb.ones_col[:, :], sb.t_bf[:, sl2],
                 start=(s == 0), stop=(s == 15))
            ev_stat[s] = p.op("tensor", "matmul", sb.pR2[0:1, :], sb.ones_col[:, :], sb.t_bf2[:, sl2],
                              start=(s == 0), stop=(s == 15), inc="PE")
        p.wait("vector", "PE", ev_stat[15])
        p.op("vector", "tensor_scalar", sb.r_mean[0:1, :], sb.pR1[0:1, :], 1.0 / D, None, ALU.mult)
        ev_prev_rows = p.op("vector", "tensor_scalar", sb.r_msq[0:1, :], sb.pR2[0:1, :], 1.0 / D, None,
                            ALU.mult, inc="DV")
        p.op("vector", "tensor_mul", sb.r_tmp[0:1, :], sb.r_mean[0:1, :], sb.r_mean[0:1, :])
        p.op("vector", "tensor_sub", sb.r_var[0:1, :], sb.r_msq[0:1, :], sb.r_tmp[0:1, :])
        p.op("vector", "tensor_scalar", sb.r_var2[0:1, :], sb.r_var[0:1, :], 1.0, LN_EPS, ALU.mult, ALU.add)
        p.op("vector", "reciprocal", sb.r_rv[0:1, :], sb.r_var2[0:1, :])
        for _ in range(8):
            p.op("vector", "engine_nop")
        ev_rv = p.op("vector", "engine_nop", inc="DV")
        p.wait("scalar", "DV", ev_rv)
        ev_r = p.op("scalar", "activation", sb.r_rstd[0:1, :], sb.r_rv[0:1, :], AF.Sqrt, inc="AC")
        p.wait("vector", "AC", ev_r)
        ev_nmr = p.op("vector", "scalar_tensor_tensor", sb.r_nmr[0:1, :], sb.r_mean[0:1, :], -1.0,
                      sb.r_rstd[0:1, :], ALU.mult, ALU.mult, inc="DV")
        p.wait("tensor", "DV", ev_nmr)
        p.wait("tensor", "DV", ev_prev_bcast_read)
        p.op("tensor", "matmul", sb.pE[:, :], sb.ones_rowf[:, :], sb.r_rstd[0:1, :], start=True, stop=True)
        ev_bc = p.op("tensor", "matmul", sb.pF[:, :], sb.ones_rowf[:, :], sb.r_nmr[0:1, :],
                     start=True, stop=True, inc="PE")
        ev_ap = [None] * 16
        ev_st = [None] * 16
        for s in range(16):
            sl2 = slice((s % 2) * 512, (s % 2) * 512 + 512)
            p.wait("vector", "PE", ev_bc)
            if s >= 2:
                p.wait("vector", "Gl0" if s % 2 == 0 else "Gl1", ev_st[s - 2])
            p.op("vector", "tensor_tensor", sb.t_f32[:, sl2], sb.xbuf[:, s * 512:(s + 1) * 512],
                 sb.pE[:, :], ALU.mult)
            ev_ap[s] = p.op("vector", "tensor_tensor", sb.t_lnout[:, sl2], sb.t_f32[:, sl2],
                            sb.pF[:, :], ALU.add, inc="DV")
            p.wait("gpsimd", "DV", ev_ap[s])
            ev_st[s] = p.dma("gpsimd", ln_dram[s * 128:(s + 1) * 128, ctt], sb.t_lnout[:, sl2],
                             "Gl0" if s % 2 == 0 else "Gl1")
        ev_prev_apply = ev_ap[15]
        ev_prev_bcast_read = ev_ap[15]
        p.wait("gpsimd", "Gl0", ev_st[14])
        p.wait("gpsimd", "Gl1", ev_st[15])
    p.barrier()



def emit_ffn(p, sb, ln_dram, x_in_dram, x_out_dram, dbg=None, cc_wait=None):
    NQ = 32
    if cc_wait:
        p.wait("sync", "CC", cc_wait)
        p.wait("gpsimd", "CC", cc_wait)
    for tt in range(NTT):
        ctt = slice(tt * 512, (tt + 1) * 512)
        p.gate()
        ev_lnall = p.dma("sync", sb.lnbuf.rearrange("p (s f) -> p s f", s=16),
                         ln_dram[:, ctt].rearrange("(s p) f -> p s f", p=128), "Wln")
        ev_xall = p.dma("scalar", sb.xbuf.rearrange("p (s f) -> p s f", s=16),
                        x_in_dram[:, ctt].rearrange("(s p) f -> p s f", p=128), "Xt")

        pair_mm = [None] * 64
        pair_silu = [None] * 64
        pair_h = [None] * 64
        pair_sq = [None] * 64
        pair_ssq = [None] * 64
        psum_g = [sb.pA, sb.pC, sb.pE]
        psum_u = [sb.pB, sb.pD, sb.pF]

        def emit_ssq(pr):
            p.wait("tensor", "AC", pair_sq[pr])
            pair_ssq[pr] = p.op("tensor", "matmul", sb.pR1[0:1, :], sb.ones_col[:, :],
                                sb.h2buf[:, (pr % 4) * 512:(pr % 4) * 512 + 512],
                                start=(pr == 0), stop=(pr == 63), inc="PE")

        for qd in range(NQ):
            buf = qd % 2
            gsl = sb.wstream[:, buf * 4096:buf * 4096 + 4096]
            usl = sb.wstream[:, 8192 + buf * 4096:8192 + buf * 4096 + 4096]
            if qd >= 2:
                p.wait("sync", "PE", pair_mm[2 * (qd - 2) + 1])
            gcol = 256 * qd
            ucol = HF + 256 * qd
            ev_wg = p.dma("sync", gsl.rearrange("p (c o) -> p c o", c=16),
                          sb.WguT_slab[gcol // 4096][:, gcol % 4096:gcol % 4096 + 256]
                          .rearrange("(c p) o -> p c o", p=128), "Wg%d" % buf)
            ev_wu = p.dma("sync", usl.rearrange("p (c o) -> p c o", c=16),
                          sb.WguT_slab[ucol // 4096][:, ucol % 4096:ucol % 4096 + 256]
                          .rearrange("(c p) o -> p c o", p=128), "Wu%d" % buf)
            if qd >= 1:
                emit_ssq(2 * (qd - 1))
                emit_ssq(2 * (qd - 1) + 1)
            p.wait("tensor", "Wg%d" % buf, ev_wg)
            p.wait("tensor", "Wu%d" % buf, ev_wu)
            if qd == 0:
                p.wait("tensor", "Wln", ev_lnall)
            for j in range(2):
                pr = 2 * qd + j
                sl3 = pr % 3
                if pr >= 3:
                    p.wait("tensor", "AC", pair_silu[pr - 3])
                    p.wait("tensor", "DV", pair_h[pr - 3])
                for c in range(16):
                    p.op("tensor", "matmul", psum_g[sl3][:, :],
                         gsl[:, c * 256 + j * 128:c * 256 + j * 128 + 128],
                         sb.lnbuf[:, c * 512:(c + 1) * 512], start=(c == 0), stop=(c == 15))
                for c in range(16):
                    pair_mm[pr] = p.op("tensor", "matmul", psum_u[sl3][:, :],
                                       usl[:, c * 256 + j * 128:c * 256 + j * 128 + 128],
                                       sb.lnbuf[:, c * 512:(c + 1) * 512], start=(c == 0), stop=(c == 15),
                                       inc="PE" if c == 15 else None)
            for j in range(2):
                pr = 2 * qd + j
                sl3 = pr % 3
                p.wait("scalar", "PE", pair_mm[pr])
                if pr >= 3:
                    p.wait("scalar", "DV", pair_h[pr - 3])
                pair_silu[pr] = p.op("scalar", "activation", sb.sgbuf[:, sl3 * 512:sl3 * 512 + 512],
                                     psum_g[sl3][:, :], AF.Silu, inc="AC")
                p.wait("vector", "AC", pair_silu[pr])
                pair_h[pr] = p.op("vector", "tensor_tensor", sb.hbuf[:, pr * 512:(pr + 1) * 512],
                                  sb.sgbuf[:, sl3 * 512:sl3 * 512 + 512], psum_u[sl3][:, :], ALU.mult, inc="DV")
                p.wait("scalar", "DV", pair_h[pr])
                if pr >= 4:
                    p.wait("scalar", "PE", pair_ssq[pr - 4])
                pair_sq[pr] = p.op("scalar", "activation", sb.h2buf[:, (pr % 4) * 512:(pr % 4) * 512 + 512],
                                   sb.hbuf[:, pr * 512:(pr + 1) * 512], AF.Square, inc="AC")
        emit_ssq(62)
        emit_ssq(63)

        p.wait("vector", "PE", pair_ssq[63])
        p.op("vector", "tensor_scalar", sb.r_ms[0:1, :], sb.pR1[0:1, :], 1.0 / HF, RMS_EPS, ALU.mult, ALU.add)
        p.op("vector", "reciprocal", sb.r_rv[0:1, :], sb.r_ms[0:1, :])
        for _ in range(8):
            p.op("vector", "engine_nop")
        ev_rv = p.op("vector", "engine_nop", inc="DV")
        p.wait("scalar", "DV", ev_rv)
        ev_rmsr = p.op("scalar", "activation", sb.r_rstd[0:1, :], sb.r_rv[0:1, :], AF.Sqrt, inc="AC")
        p.wait("tensor", "AC", ev_rmsr)
        p.wait("tensor", "DV", pair_h[62])
        ev_rmsbc = p.op("tensor", "matmul", sb.pE[:, :], sb.ones_rowf[:, :], sb.r_rstd[0:1, :],
                        start=True, stop=True, inc="PE")
        p.wait("vector", "PE", ev_rmsbc)
        p.op("vector", "tensor_copy", sb.rms_bs[:, :], sb.pE[:, :])
        if dbg is not None and tt == 0:
            p.gate()
            p.dma("gpsimd", dbg["h"].ap(), sb.hbuf[:, :], "Gd")
            p.dma("gpsimd", dbg["ln"].ap(), sb.lnbuf[:, :], "Gd")
            p.dma("gpsimd", dbg["rms"].ap(), sb.rms_bs[:, :], "Gd")
            p.wait("gpsimd", "Gd", p.ctr["Gd"])
            p.gate()

        pbank = [sb.pA, sb.pB, sb.pC, sb.pD]
        slab_pe = {}
        ev_evac = {}
        ev_store = {}
        p.wait("sync", "PE", pair_mm[63])
        idx = 0
        for s4 in range(4):
            for hc in range(64):
                wbuf = idx % 8
                slab = sb.wstream[:, wbuf * 512:wbuf * 512 + 512]
                if idx >= 8:
                    p.wait("sync", "PE", slab_pe[idx - 8])
                ev_w = p.dma("sync", slab, sb.WpT[hc * 128:(hc + 1) * 128, s4 * 512:s4 * 512 + 512],
                             "Wp%d" % wbuf)
                p.wait("tensor", "Wp%d" % wbuf, ev_w)
                if hc == 0 and s4 > 0:
                    p.wait("tensor", "DV", ev_evac[s4 * 4 - 1])
                if hc == 0 and s4 == 0:
                    p.wait("tensor", "AC", pair_silu[63])
                    p.wait("tensor", "DV", pair_h[63])
                for sl in range(4):
                    slab_pe[idx] = p.op("tensor", "matmul", pbank[sl][:, :],
                                        slab[:, sl * 128:sl * 128 + 128],
                                        sb.hbuf[:, hc * 512:(hc + 1) * 512],
                                        start=(hc == 0), stop=(hc == 63),
                                        inc="PE" if sl == 3 else None)
                idx += 1
            for sl in range(4):
                s = s4 * 4 + sl
                sl2 = slice((s % 2) * 512, (s % 2) * 512 + 512)
                p.wait("vector", "PE", slab_pe[idx - 1])
                if s == 0:
                    p.wait("vector", "Xt", ev_xall)
                if s >= 2:
                    p.wait("vector", "Go%d" % (s % 2), ev_store[s - 2])
                p.op("vector", "tensor_tensor", sb.t_f32[:, sl2], pbank[sl][:, :], sb.rms_bs[:, :], ALU.mult)
                ev_evac[s] = p.op("vector", "scalar_tensor_tensor", sb.xo[:, sl2], sb.t_f32[:, sl2], 0.5,
                                  sb.xbuf[:, s * 512:(s + 1) * 512], ALU.mult, ALU.add, inc="DV")
                p.wait("gpsimd", "DV", ev_evac[s])
                ev_store[s] = p.dma("gpsimd", x_out_dram[s * 128:(s + 1) * 128, ctt], sb.xo[:, sl2],
                                    "Go%d" % (s % 2))
        if dbg is not None and tt == 0:
            p.gate()
            p.dma("gpsimd", dbg["h2"].ap(), sb.hbuf[:, :], "Gd")
            p.wait("gpsimd", "Gd", p.ctr["Gd"])
            p.gate()
        p.wait("gpsimd", "Go0", ev_store[14])
        p.wait("gpsimd", "Go1", ev_store[15])
    p.barrier()



def emit_attn(p, sb):
    qk_psums = [sb.pA, sb.pB, sb.pC, sb.pD]
    pending_lazy = [None]
    ev_norm = [None, None]
    for hg in range(NH // 2):
        p.gate()
        base = hg * 256
        p.dma("sync", sb.wstream[:, 0:4096].rearrange("p (c o) -> p c o", c=16),
              sb.WqkvT[:, base:base + 256].rearrange("(c p) o -> p c o", p=128), "Wq")
        p.dma("sync", sb.wstream[:, 4096:8192].rearrange("p (c o) -> p c o", c=16),
              sb.WqkvT[:, D + base:D + base + 256].rearrange("(c p) o -> p c o", p=128), "Wq")
        ev_wv = p.dma("sync", sb.wstream[:, 8192:12288].rearrange("p (c o) -> p c o", c=16),
                      sb.WqkvT[:, 2 * D + base:2 * D + base + 256].rearrange("(c p) o -> p c o", p=128), "Wq")
        p.wait("tensor", "Wq", ev_wv)

        ev_chunk = {}
        ev_qkevac = None
        for tg in range(4):
            ev_mm = None
            for c in range(16):
                i = tg * 16 + c
                lbuf = i % 4
                lsl = sb.lnsl[:, lbuf * 512:lbuf * 512 + 512]
                if i >= 4:
                    p.wait("sync", "PE", ev_chunk[i - 4])
                ev_ln = p.dma("sync", lsl, sb.ln2T[c * 128:(c + 1) * 128, tg * 512:(tg + 1) * 512],
                              "Wl%d" % lbuf)
                p.wait("tensor", "Wl%d" % lbuf, ev_ln)
                if c == 0 and tg > 0:
                    p.wait("tensor", "DV", ev_qkevac)
                for sl in range(4):
                    wofs = [0, 4096, 0, 4096][sl] + c * 256 + (sl // 2) * 128
                    p.op("tensor", "matmul", qk_psums[sl][:, :], sb.wstream[:, wofs:wofs + 128],
                         lsl, start=(c == 0), stop=(c == 15))
                for ts in range(4):
                    vp = [sb.pE, sb.pF][ts // 2][:, (ts % 2) * 256:(ts % 2) * 256 + 256]
                    ev_mm = p.op("tensor", "matmul", vp,
                                 lsl[:, ts * 128:ts * 128 + 128],
                                 sb.wstream[:, 8192 + c * 256:8192 + c * 256 + 256],
                                 start=(c == 0 and ts % 2 == 0), stop=(c == 15),
                                 inc="PE" if ts == 3 else None)
                ev_chunk[i] = ev_mm
            p.wait("vector", "PE", ev_mm)
            for sl in range(4):
                p.op("vector", "tensor_copy",
                     sb.lnbuf[:, sl * 2048 + tg * 512:sl * 2048 + (tg + 1) * 512], qk_psums[sl][:, :])
            ev_qkevac = None
            for ts in range(4):
                tsub = tg * 4 + ts
                pp = [sb.pE, sb.pF][ts // 2]
                for j in range(2):
                    ev_qkevac = p.op("vector", "tensor_copy",
                                     sb.vbuf[:, j * 2048 + tsub * 128:j * 2048 + (tsub + 1) * 128],
                                     pp[:, (ts % 2) * 256 + j * 128:(ts % 2) * 256 + j * 128 + 128],
                                     inc="DV" if (ts == 3 and j == 1) else None)
        p.gate()

        for j in range(2):
            if pending_lazy[0] is not None:
                fn = pending_lazy[0]
                pending_lazy[0] = None
                fn()
            p.gate(engines=("scalar", "vector", "tensor"))
            h = 2 * hg + j
            Q = sb.lnbuf[:, (2 * j) * 2048:(2 * j) * 2048 + 2048]
            K = sb.lnbuf[:, (2 * j + 1) * 2048:(2 * j + 1) * 2048 + 2048]
            V = sb.vbuf[:, j * 2048:(j + 1) * 2048]
            for src, dst in ((Q, sb.r_qn2), (K, sb.r_kn2)):
                for i4 in range(4):
                    ssl = slice((i4 % 2) * 512, (i4 % 2) * 512 + 512)
                    if True:
                        pass
                    ev_sq = p.op("scalar", "activation", sb.pe_scr[:, ssl], src[:, i4 * 512:(i4 + 1) * 512],
                                 AF.Square, inc="AC")
                    p.wait("tensor", "AC", ev_sq)
                    ev_n = p.op("tensor", "matmul", sb.pR2[0:1, :], sb.ones_col[:, :], sb.pe_scr[:, ssl],
                                start=True, stop=True, inc="PE")
                    p.wait("vector", "PE", ev_n)
                    ev_cp = p.op("vector", "tensor_copy", dst[0:1, i4 * 512:(i4 + 1) * 512],
                                 sb.pR2[0:1, :], inc="DV")
                    p.wait("scalar", "DV", ev_cp)
            p.op("vector", "tensor_reduce", sb.r_sc1[0:1, 0:1], sb.r_qn2[0:1, :], AXX, ALU.max)
            p.op("vector", "tensor_reduce", sb.r_sc2[0:1, 0:1], sb.r_kn2[0:1, :], AXX, ALU.max)
            for _ in range(8):
                p.op("vector", "engine_nop")
            ev_red = p.op("vector", "engine_nop", inc="DV")
            p.wait("gpsimd", "DV", ev_red)
            p.op("gpsimd", "tensor_add", sb.r_sc3[0:1, 0:1], sb.r_sc1[0:1, 0:1], sb.r_sc2[0:1, 0:1])
            ev_nb = p.op("gpsimd", "tensor_scalar_mul", sb.r_sc3[0:1, 0:1], sb.r_sc3[0:1, 0:1],
                         -SM_SCALE / 2, inc="GP")
            p.wait("tensor", "GP", ev_nb)
            ev_nbc = p.op("tensor", "matmul", sb.pC[:, 0:1], sb.ones_rowf[:, :], sb.r_sc3[0:1, 0:1],
                          start=True, stop=True, inc="PE")
            p.wait("vector", "PE", ev_nbc)
            ev_nbcol = p.op("vector", "tensor_copy", sb.nbcol[:, :], sb.pC[:, 0:1], inc="DV")
            p.wait("scalar", "DV", ev_nbcol)
            p.wait("tensor", "DV", ev_nbcol)

            ev_exp_prev_qg = None
            ev_stage_prev = [None]
            for qg in range(4):
                nvis = 4 * qg + 4
                ev_exp = [None] * 64
                ev_pt = [None] * 64
                ev_lmm = [None] * 64
                banks = [sb.pA, sb.pB, sb.pD, sb.pF, sb.pE]

                def emit_pv(kc, last):
                    sem, val = ev_pt[kc]
                    p.wait("tensor", sem, val)
                    p.op("tensor", "matmul", sb.pC[:, :], V[:, kc * 128:(kc + 1) * 128],
                         sb.ptbuf[:, (kc % 4) * 512:(kc % 4) * 512 + 512],
                         start=(kc == 0), stop=last)
                    ev_lmm[kc] = p.op("tensor", "matmul", sb.pR1[0:1, :], sb.ones_col[:, :],
                                      sb.ptbuf[:, (kc % 4) * 512:(kc % 4) * 512 + 512],
                                      start=(kc == 0), stop=last, inc="PE")

                for kc in range(nvis):
                    pb = (kc + qg) % 5
                    psc = banks[pb]
                    if kc >= 5:
                        p.wait("tensor", "AC", ev_exp[kc - 5])
                    elif ev_exp_prev_qg:
                        p.wait("tensor", "AC", ev_exp_prev_qg)
                    if kc == 4 and qg >= 1 and ev_norm[(qg - 1) % 2] is not None:
                        p.wait("tensor", "DV", ev_norm[(qg - 1) % 2])
                    ev_sc = p.op("tensor", "matmul", psc[:, :], K[:, kc * 128:(kc + 1) * 128],
                                 Q[:, qg * 512:(qg + 1) * 512], start=True, stop=True, inc="PE")
                    p.wait("scalar", "PE", ev_sc)
                    diag = kc - 4 * qg
                    sl4 = kc % 4
                    if diag >= 0:
                        if kc >= 4:
                            p.wait("scalar", ev_pt[kc - 4][0], ev_pt[kc - 4][1])
                        ev_e = p.op("scalar", "activation", sb.pe_scr[:, sl4 * 512:sl4 * 512 + 512],
                                    psc[:, :], AF.Exp, bias=sb.nbcol[:, :], scale=SM_SCALE, inc="AC")
                        ev_exp[kc] = ev_e
                        p.wait("vector", "AC", ev_e)
                        if kc >= 4:
                            p.wait("vector", "PE", ev_lmm[kc - 4])
                        ev_pt[kc] = ("DV", p.op("vector", "tensor_tensor",
                                                sb.ptbuf[:, sl4 * 512:sl4 * 512 + 512],
                                                sb.pe_scr[:, sl4 * 512:sl4 * 512 + 512],
                                                sb.maskbuf[:, diag * 512:(diag + 1) * 512],
                                                ALU.mult, inc="DV"))
                    else:
                        if kc >= 4:
                            p.wait("scalar", "PE", ev_lmm[kc - 4])
                        ev_e = p.op("scalar", "activation", sb.ptbuf[:, sl4 * 512:sl4 * 512 + 512],
                                    psc[:, :], AF.Exp, bias=sb.nbcol[:, :], scale=SM_SCALE, inc="AC")
                        ev_exp[kc] = ev_e
                        ev_pt[kc] = ("AC", ev_e)
                    if kc >= 2 and pending_lazy[0] is not None:
                        fn = pending_lazy[0]
                        pending_lazy[0] = None
                        fn()
                    if kc >= 3:
                        if kc == 3 and ev_stage_prev[0] is not None:
                            p.wait("tensor", "DV", ev_stage_prev[0])
                        emit_pv(kc - 3, last=False)
                for tail in (3, 2, 1):
                    emit_pv(nvis - tail, last=(tail == 1))
                ev_l = ev_lmm[nvis - 1]
                ev_exp_prev_qg = ev_exp[nvis - 1]
                stage = sb.t_bf[:, (qg % 2) * 512:(qg % 2) * 512 + 512]
                p.wait("vector", "PE", ev_l)
                if ev_norm[qg % 2] is not None:
                    p.wait("vector", "DV", ev_norm[qg % 2])
                p.op("vector", "tensor_copy", stage, sb.pC[:, :])
                p.op("vector", "tensor_copy", sb.r_l[0:1, :], sb.pR1[0:1, :])
                ev_stage = p.op("vector", "engine_nop", inc="DV")
                ev_stage_prev[0] = ev_stage

                def lazy(qg=qg, h=h, stage=stage):
                    p.op("vector", "reciprocal", sb.r_rl[0:1, :], sb.r_l[0:1, :])
                    for _ in range(8):
                        p.op("vector", "engine_nop")
                    ev_rl = p.op("vector", "engine_nop", inc="DV")
                    p.wait("tensor", "DV", ev_rl)
                    rlbank = banks[qg % 5]
                    ev_rlbc = p.op("tensor", "matmul", rlbank[:, :], sb.ones_rowf[:, :],
                                   sb.r_rl[0:1, :], start=True, stop=True, inc="PE")
                    p.wait("vector", "PE", ev_rlbc)
                    p.op("vector", "tensor_copy", sb.rls[:, :], rlbank[:, :])
                    ev_norm[qg % 2] = p.op("vector", "tensor_tensor",
                                           sb.hbuf[:, h * 2048 + qg * 512:h * 2048 + (qg + 1) * 512],
                                           stage, sb.rls[:, :], ALU.mult, inc="DV")
                pending_lazy[0] = lazy

    if pending_lazy[0] is not None:
        fn = pending_lazy[0]
        pending_lazy[0] = None
        fn()
    pbank = [sb.pA, sb.pB, sb.pC, sb.pD]
    for tt in range(NTT):
        ctt = slice(tt * 512, (tt + 1) * 512)
        for s4 in range(4):
            p.gate()
            ev_x1 = p.dma("scalar", sb.xbuf[:, 0:2048].rearrange("p (s f) -> p s f", s=4),
                          sb.x1T[s4 * 512:(s4 + 1) * 512, ctt].rearrange("(s p) f -> p s f", p=128),
                          "Xt")
            slab_pe = {}
            for ac in range(16):
                wbuf = ac % 8
                slab = sb.wstream[:, wbuf * 512:wbuf * 512 + 512]
                if ac >= 8:
                    p.wait("sync", "PE", slab_pe[ac - 8])
                ev_w = p.dma("sync", slab, sb.WoT[ac * 128:(ac + 1) * 128, s4 * 512:s4 * 512 + 512],
                             "Wp%d" % wbuf)
                p.wait("tensor", "Wp%d" % wbuf, ev_w)
                for sl in range(4):
                    slab_pe[ac] = p.op("tensor", "matmul", pbank[sl][:, :],
                                       slab[:, sl * 128:sl * 128 + 128],
                                       sb.hbuf[:, ac * 2048 + tt * 512:ac * 2048 + (tt + 1) * 512],
                                       start=(ac == 0), stop=(ac == 15),
                                       inc="PE" if sl == 3 else None)
            ev_store = {}
            for sl in range(4):
                s = s4 * 4 + sl
                sl2 = slice((sl % 2) * 512, (sl % 2) * 512 + 512)
                p.wait("vector", "PE", slab_pe[15])
                if sl == 0:
                    p.wait("vector", "Xt", ev_x1)
                if sl >= 2:
                    p.wait("vector", "Go%d" % (sl % 2), ev_store[sl - 2])
                ev_e = p.op("vector", "tensor_tensor", sb.xo[:, sl2], pbank[sl][:, :],
                            sb.xbuf[:, sl * 512:(sl + 1) * 512], ALU.add, inc="DV")
                p.wait("gpsimd", "DV", ev_e)
                ev_store[sl] = p.dma("gpsimd", sb.x2T[s * 128:(s + 1) * 128, ctt], sb.xo[:, sl2],
                                     "Go%d" % (sl % 2))
            p.wait("gpsimd", "Go0", ev_store[2])
            p.wait("gpsimd", "Go1", ev_store[3])
    p.barrier()



def build_nc(debug_outputs=()):
    nc = bass.Bass("TRN2", target_bir_lowering=False, detect_race_conditions=False)

    class Ctx:
        pass

    sb = Ctx()
    sb.x0T = nc.dram_tensor("x0T", [D, S], F32, kind="ExternalInput")
    sb.wgu_sh = nc.dram_tensor("wgu_sh", [256, 2 * HF], F32, kind="ExternalInput")
    sb.wp_sh = nc.dram_tensor("wp_sh", [1024, D], F32, kind="ExternalInput")
    sb.wqkv_sh = nc.dram_tensor("wqkv_sh", [256, 3 * D], F32, kind="ExternalInput")
    sb.wo_sh = nc.dram_tensor("wo_sh", [256, D], F32, kind="ExternalInput")
    sb.yT = nc.dram_tensor("yT", [D, S], F32, kind="ExternalOutput")

    def internal(name, shape, dt):
        kind = "ExternalOutput" if name in debug_outputs else "Internal"
        return nc.dram_tensor(name, shape, dt, kind=kind)

    sb.q_wgu_slab = [internal(f"q_wgu_s{k}", [256, 4096], BF16) for k in range(4)]
    sb.WguT_slab = [internal(f"WguT_s{k}", [D, 4096], BF16) for k in range(4)]
    sb.q_wp_sh = internal("q_wp_sh", [1024, D], BF16)
    sb.b_wqkv_sh = internal("b_wqkv_sh", [256, 3 * D], BF16)
    sb.b_wo_sh = internal("b_wo_sh", [256, D], BF16)

    sb.WpT = internal("WpT", [HF, D], BF16)
    sb.WqkvT = internal("WqkvT", [D, 3 * D], BF16)
    sb.WoT = internal("WoT", [D, D], BF16)
    sb.dbg_h = internal("dbg_h", [128, 32768], BF16)
    sb.dbg_h2 = internal("dbg_h2", [128, 32768], BF16)
    sb.dbg_ln = internal("dbg_ln", [128, 8192], BF16)
    sb.dbg_rms = internal("dbg_rms", [128, 512], F32)
    sb.x1T = internal("x1T", [D, S], F32)
    sb.x2T = internal("x2T", [D, S], F32)
    sb.ln1T = internal("ln1T", [D, S], BF16)
    sb.ln2T = internal("ln2T", [D, S], BF16)
    sb.ln3T = internal("ln3T", [D, S], BF16)

    selb_np = np.zeros((128, 2), dtype=BF)
    selb_np[:64, 0] = 1
    selb_np[64:, 1] = 1
    selk2_np = np.zeros((2, 128), dtype=np.float32)
    selk2_np[0, :64] = 1
    selk2_np[1, 64:] = 1
    ones_col_np = np.ones((128, 1), dtype=BF)
    ones_rowf_np = np.ones((1, 128), dtype=np.float32)
    masks_np = np.zeros((4 * 128, 512), dtype=BF)
    for jj in range(4):
        for pp in range(128):
            masks_np[jj * 128 + pp, 128 * jj + pp:] = 1
    C_SELB = nc.inline_tensor(selb_np, "c_selb")
    C_SELK2 = nc.inline_tensor(selk2_np, "c_selk2")
    C_ONESC = nc.inline_tensor(ones_col_np, "c_onesc")
    C_ONESRF = nc.inline_tensor(ones_rowf_np, "c_onesrf")
    C_MASKS = nc.inline_tensor(masks_np, "c_masks")

    with ExitStack() as stack:
        def sbuf(name, shape, dt):
            return stack.enter_context(nc.sbuf_tensor(name, shape, dt))

        def psum(name, shape, dt):
            return stack.enter_context(nc.psum_tensor(name, shape, dt))

        sb.lnbuf = sbuf("lnbuf", [128, 8192], BF16)
        sb.hbuf = sbuf("hbuf", [128, 32768], BF16)
        sb.xbuf = sbuf("xbuf", [128, 8192], F32)
        sb.wstream = sbuf("wstream", [128, 16384], BF16)
        sb.vbuf = sbuf("vbuf", [128, 4096], BF16)
        sb.sgbuf = sbuf("sgbuf", [128, 1536], F32)
        sb.h2buf = None
        sb.t_bf = sbuf("t_bf", [128, 2048], BF16)
        sb.t_bf2 = sbuf("t_bf2", [128, 2048], BF16)
        sb.t_f32 = sbuf("t_f32", [128, 1024], F32)
        sb.t_lnout = None
        sb.xo = sbuf("xo", [128, 1024], F32)
        sb.rms_bs = sbuf("rms_bs", [128, 512], F32)
        sb.rls = sbuf("rls", [128, 512], F32)
        sb.ptbuf = sbuf("ptbuf", [128, 2048], BF16)
        sb.pe_scr = sbuf("pe_scr", [128, 2048], BF16)
        sb.h2buf = sb.ptbuf
        sb.t_lnout = sb.pe_scr[:, 1024:2048]
        sb.lnsl = sbuf("lnsl", [128, 2048], BF16)
        sb.maskbuf = sbuf("maskbuf", [128, 2048], BF16)
        sb.nbcol = sbuf("nbcol", [128, 1], F32)
        sb.q_scaleb = sbuf("q_scaleb", [2, 512], BF16)
        sb.q_scaleb2 = sbuf("q_scaleb2", [2, 512], BF16)
        sb.q_scalef = sb.sgbuf[0:2, 512:1024]
        sb.q_thr = sb.sgbuf[0:2, 0:512]
        sb.r_mean = sbuf("r_mean", [1, 512], F32)
        sb.r_msq = sbuf("r_msq", [1, 512], F32)
        sb.r_rv = sbuf("r_rv", [1, 512], F32)
        sb.r_rstd = sbuf("r_rstd", [1, 512], F32)
        sb.r_tmp = sb.sgbuf[0:1, 1024:1536]
        sb.r_var = sb.r_rstd
        sb.r_nmr = sb.r_rv
        sb.r_var2 = sb.r_tmp
        sb.r_ms = sb.r_var
        sb.r_l = sb.r_mean
        sb.r_rl = sb.r_msq
        sb.q_w32 = sb.t_f32[:, 0:512]
        sb.q_wb = sb.t_bf[:, 0:512]
        sb.q_d = sb.t_bf[:, 512:1024]
        sb.q_aw = sb.t_bf2[:, 0:512]
        sb.q_sgn = sb.t_lnout[:, 0:512]
        sb.q_m = sb.ptbuf[:, 0:512]
        sb.q_out = sb.ptbuf[:, 512:1024]
        sb.q_q = sb.pe_scr[:, 0:512]
        sb.q_what = sb.lnsl[:, 0:512]
        sb.r_qn2 = sb.xbuf[0:1, 0:2048]
        sb.r_kn2 = sb.xbuf[0:1, 2048:4096]
        sb.r_sc1 = sbuf("r_sc1", [1, 1], F32)
        sb.r_sc2 = sbuf("r_sc2", [1, 1], F32)
        sb.r_sc3 = sbuf("r_sc3", [1, 1], F32)
        sb.selb = sbuf("selb", [128, 2], BF16)
        sb.selk2 = sbuf("selk2", [2, 128], F32)
        sb.ones_col = sbuf("ones_col", [128, 1], BF16)
        sb.ones_rowf = sbuf("ones_rowf", [1, 128], F32)
        sb.pA = psum("pA", [128, 512], F32)
        sb.pB = psum("pB", [128, 512], F32)
        sb.pC = psum("pC", [128, 512], F32)
        sb.pD = psum("pD", [128, 512], F32)
        sb.pE = psum("pE", [128, 512], F32)
        sb.pF = psum("pF", [128, 512], F32)
        sb.pR1 = psum("pR1", [2, 512], F32)
        sb.pR2 = psum("pR2", [2, 512], F32)

        sems = {n: stack.enter_context(nc.semaphore("s" + n)) for n in SEMNAMES}
        p = P(sems)

        for dst, src in ((sb.selb, C_SELB), (sb.selk2, C_SELK2),
                         (sb.ones_col, C_ONESC), (sb.ones_rowf, C_ONESRF)):
            p.dma("sync", dst[:, :], src[:, :], "Wc")
        for jj in range(4):
            p.dma("sync", sb.maskbuf[:, jj * 512:(jj + 1) * 512], C_MASKS[jj * 128:(jj + 1) * 128, :], "Wc")
        p.gate()

        cc_target = emit_prep(p, sb)
        emit_ln_pass(p, sb, sb.x0T, sb.ln1T)
        dbg = ({"h": sb.dbg_h, "h2": sb.dbg_h2, "ln": sb.dbg_ln, "rms": sb.dbg_rms}
               if "dbg_h" in debug_outputs else None)
        emit_ffn(p, sb, sb.ln1T, sb.x0T, sb.x1T, dbg=dbg, cc_wait=cc_target)
        emit_ln_pass(p, sb, sb.x1T, sb.ln2T)
        emit_attn(p, sb)
        emit_ln_pass(p, sb, sb.x2T, sb.ln3T)
        emit_ffn(p, sb, sb.ln3T, sb.x2T, sb.yT)

        with nc.Block() as block:
            for ename in ENGINES:
                def make_body(ename):
                    def body(eng):
                        for f in p.ops[ename]:
                            f(eng)
                    return body
                getattr(block, ename)(make_body(ename))

    return nc


def kernel(L0_attn_hidden_states, L0_attn_w_qkv, L0_attn_w_o, L0_ffn_w_gate_up, L0_ffn_w_proj):
    hs = np.asarray(L0_attn_hidden_states, dtype=np.float32)
    w_qkv = np.asarray(L0_attn_w_qkv, dtype=np.float32)
    w_o = np.asarray(L0_attn_w_o, dtype=np.float32)
    w_gu = np.asarray(L0_ffn_w_gate_up, dtype=np.float32)
    w_p = np.asarray(L0_ffn_w_proj, dtype=np.float32)

    nc = build_nc()
    in_maps = []
    for c in range(NC):
        in_maps.append({
            "x0T": np.ascontiguousarray(hs[c].T),
            "wgu_sh": np.ascontiguousarray(w_gu[:, c * 256:(c + 1) * 256].T),
            "wp_sh": np.ascontiguousarray(w_p[:, c * 1024:(c + 1) * 1024].T),
            "wqkv_sh": np.ascontiguousarray(w_qkv[:, c * 256:(c + 1) * 256].T),
            "wo_sh": np.ascontiguousarray(w_o[:, c * 256:(c + 1) * 256].T),
        })
    res = run_bass_kernel_spmd(nc, in_maps, core_ids=list(range(NC)))
    return np.stack([np.ascontiguousarray(r["yT"].T) for r in res.results])


# version 09
# speedup vs baseline: 1.0193x; 1.0168x over previous
import math
import numpy as np
import ml_dtypes
from contextlib import ExitStack

import concourse.bass as bass
import concourse.mybir as mybir
from concourse.bass_utils import run_bass_kernel_spmd

F32 = mybir.dt.float32
BF16 = mybir.dt.bfloat16
AF = mybir.ActivationFunctionType
ALU = mybir.AluOpType
AXX = mybir.AxisListType.X

BF = ml_dtypes.bfloat16

D = 2048
S = 2048
NH = 16
HD = 128
HF = 8192
NC = 8
TT = 512
NTT = S // TT
SM_SCALE = 1.0 / math.sqrt(HD)
THR_C = 0.5 * (1.0 + 2.0 ** -9)
LN_EPS = 1e-5
RMS_EPS = float(np.finfo(np.float32).eps)

ENGINES = ("sync", "vector", "scalar", "tensor", "gpsimd")
SEMNAMES = (["Wg0", "Wg1", "Wu0", "Wu1"] + [f"Wp{i}" for i in range(8)]
            + ["Wl0", "Wl1", "Wl2", "Wl3", "Wq", "Wln", "Wz", "Wz2", "Wc", "Xt",
               "Go0", "Go1", "Gl0", "Gl1", "Gq", "Gq2", "Gd",
               "PE", "DV", "AC", "GP", "CC", "B"])
DMA_SEMS = tuple(n for n in SEMNAMES if n[0] in "WXG" and n != "GP")


class P:

    def __init__(self, sems):
        self.ops = {e: [] for e in ENGINES}
        self.sems = sems
        self.ctr = {name: 0 for name in sems}

    def op(self, engine, method, *args, inc=None, inc_n=1, **kwargs):
        val = None
        if inc is not None:
            self.ctr[inc] += inc_n
            val = self.ctr[inc]
            sem = self.sems[inc]

            def fn(eng, method=method, args=args, kwargs=kwargs, sem=sem, n=inc_n):
                getattr(eng, method)(*args, **kwargs).then_inc(sem, n)
        else:
            def fn(eng, method=method, args=args, kwargs=kwargs):
                getattr(eng, method)(*args, **kwargs)
        self.ops[engine].append(fn)
        return val

    def dma(self, engine, out, in_, sem):
        return self.op(engine, "dma_start", inc=sem, inc_n=16, out=out, in_=in_)

    def wait(self, engine, semname, val):
        if not val:
            return
        sem = self.sems[semname]

        def fn(eng, sem=sem, val=val):
            eng.wait_ge(sem, val)
        self.ops[engine].append(fn)

    def gate(self, engines=ENGINES, sems=None):
        if sems is None:
            sems = [n for n in SEMNAMES if n not in ("B", "CC")]
        snap = {k: self.ctr[k] for k in sems}
        own = {"sync": None, "vector": "DV", "scalar": "AC", "tensor": "PE", "gpsimd": "GP"}
        for e in engines:
            for k, v in snap.items():
                if v and own.get(e) != k:
                    self.wait(e, k, v)

    def barrier(self):
        self.ctr["B"] += len(ENGINES)
        target = self.ctr["B"]
        sem = self.sems["B"]
        for e in ENGINES:
            def fn(eng, sem=sem, target=target):
                eng.drain()
                eng.nop().then_inc(sem, 1)
                eng.wait_ge(sem, target)
            self.ops[e].append(fn)



def emit_quant_shard(p, sb, src, dst, R, C):
    n_r, n_c = R // 128, C // 512
    pr_asum = [sb.pR1, sb.pR2]
    pr_thr = [sb.pE, sb.pA]
    pr_scale = [sb.pF, sb.pB]
    last = {}
    for i in range(n_r * n_c):
        r, c = divmod(i, n_c)
        sl = i % 2
        rr = slice(r * 128, (r + 1) * 128)
        cc = slice(c * 512, (c + 1) * 512)
        ldsem = ("Wz", "Wz2")[sl]
        stsem = ("Gq", "Gq2")[sl]
        p.wait("sync", "DV", last.get(("cast", sl)))
        ev_ld = p.dma("sync", sb.q_w32[sl], src[rr, cc], ldsem)
        p.wait("vector", ldsem, ev_ld)
        p.wait("vector", "AC", last.get(("absgn", sl)))
        ev_cast = p.op("vector", "tensor_copy", sb.q_wb[sl], sb.q_w32[sl], inc="DV")
        last[("cast", sl)] = ev_cast
        p.wait("scalar", "DV", ev_cast)
        p.wait("scalar", "PE", last.get(("asum", sl)))
        p.wait("scalar", "DV", last.get(("out", sl)))
        p.op("scalar", "activation", sb.q_aw[sl], sb.q_wb[sl], AF.Abs)
        ev_absgn = p.op("scalar", "activation", sb.q_sgn[sl], sb.q_wb[sl], AF.Sign, inc="AC")
        last[("absgn", sl)] = ev_absgn
        p.wait("tensor", "AC", ev_absgn)
        p.wait("tensor", "AC", last.get(("scaleb", sl)))
        ev_asum = p.op("tensor", "matmul", pr_asum[sl][:, :], sb.selb[:, :], sb.q_aw[sl],
                       start=True, stop=True, inc="PE")
        last[("asum", sl)] = ev_asum
        p.wait("scalar", "PE", ev_asum)
        p.wait("scalar", "DV", last.get(("thr", sl)))
        ev_scaleb = p.op("scalar", "activation", sb.q_scaleb[sl], pr_asum[sl][:, :], AF.Copy,
                         bias=0.0, scale=1.0 / 64, inc="AC")
        last[("scaleb", sl)] = ev_scaleb
        p.wait("vector", "AC", ev_scaleb)
        p.wait("vector", "PE", last.get(("bcast", sl)))
        p.op("vector", "tensor_scalar_max", sb.q_scalef[sl], sb.q_scaleb[sl], 1e-8)
        ev_thr = p.op("vector", "tensor_scalar_mul", sb.q_thr[sl], sb.q_scalef[sl], THR_C, inc="DV")
        last[("thr", sl)] = ev_thr
        p.wait("tensor", "DV", ev_thr)
        p.wait("tensor", "DV", last.get(("out", sl)))
        p.op("tensor", "matmul", pr_thr[sl][:, :], sb.selk2[:, :], sb.q_thr[sl], start=True, stop=True)
        ev_bc = p.op("tensor", "matmul", pr_scale[sl][:, :], sb.selk2[:, :], sb.q_scalef[sl],
                     start=True, stop=True, inc="PE")
        last[("bcast", sl)] = ev_bc
        p.wait("vector", "PE", ev_bc)
        p.wait("vector", stsem, last.get(("store", sl)))
        p.op("vector", "tensor_tensor", sb.q_m[sl], sb.q_aw[sl], pr_thr[sl][:, :], ALU.is_gt)
        p.op("vector", "tensor_mul", sb.q_q[sl], sb.q_m[sl], sb.q_sgn[sl])
        p.op("vector", "tensor_tensor", sb.q_what[sl], sb.q_q[sl], pr_scale[sl][:, :], ALU.mult)
        p.op("vector", "tensor_sub", sb.q_d[sl], sb.q_what[sl], sb.q_wb[sl])
        ev_out = p.op("vector", "tensor_add", sb.q_out[sl], sb.q_wb[sl], sb.q_d[sl], inc="DV")
        last[("out", sl)] = ev_out
        p.wait("gpsimd", "DV", ev_out)
        last[("store", sl)] = p.dma("gpsimd", dst[rr, cc], sb.q_out[sl], stsem)
    p.wait("gpsimd", "Gq", last.get(("store", 0)))
    p.wait("gpsimd", "Gq2", last.get(("store", 1)))
    return None


def emit_cast_shard(p, sb, src, dst, R, C):
    n_r, n_c = R // 128, C // 512
    evc = {}
    evs = {}
    ev_store = None
    for i in range(n_r * n_c):
        r, c = divmod(i, n_c)
        sl = i % 2
        ldsem = ("Wz", "Wz2")[sl]
        stsem = ("Gq", "Gq2")[sl]
        rr = slice(r * 128, (r + 1) * 128)
        cc = slice(c * 512, (c + 1) * 512)
        p.wait("sync", "DV", evc.get(sl))
        ev_ld = p.dma("sync", sb.q_w32[sl], src[rr, cc], ldsem)
        p.wait("vector", ldsem, ev_ld)
        p.wait("vector", stsem, evs.get(sl))
        evc[sl] = p.op("vector", "tensor_copy", sb.q_wb[sl], sb.q_w32[sl], inc="DV")
        p.wait("gpsimd", "DV", evc[sl])
        evs[sl] = p.dma("gpsimd", dst[rr, cc], sb.q_wb[sl], stsem)
    p.wait("gpsimd", "Gq", evs.get(0))
    p.wait("gpsimd", "Gq2", evs.get(1))
    return None


def emit_prep(p, sb):
    for fn, srcw, dstq, dstg, R, C in (
            (emit_cast_shard, sb.wo_sh, sb.b_wo_sh, sb.WoT, 256, D),
            (emit_cast_shard, sb.wqkv_sh, sb.b_wqkv_sh, sb.WqkvT, 256, 3 * D),
            (emit_quant_shard, sb.wp_sh, sb.q_wp_sh, sb.WpT, 1024, D)):
        fn(p, sb, srcw, dstq, R, C)
        p.op("gpsimd", "collective_compute", "AllGather", ALU.bypass,
             [list(range(NC))], [dstq.ap()], [dstg.ap()], inc="CC")
        p.gate(engines=("sync", "vector", "scalar", "tensor"))
    for k in range(4):
        emit_quant_shard(p, sb, sb.wgu_sh[:, k * 4096:(k + 1) * 4096],
                         sb.q_wgu_slab[k], 256, 4096)
        p.op("gpsimd", "collective_compute", "AllGather", ALU.bypass,
             [list(range(NC))], [sb.q_wgu_slab[k].ap()], [sb.WguT_slab[k].ap()], inc="CC")
        p.gate(engines=("sync", "vector", "scalar", "tensor"))
    return p.ctr["CC"]



def emit_ln_pass(p, sb, x_dram, ln_dram):
    ev_prev_apply = None
    ev_prev_rows = None
    ev_prev_bcast_read = None
    for tt in range(NTT):
        ctt = slice(tt * 512, (tt + 1) * 512)
        if tt == 0:
            p.gate()
        p.wait("sync", "DV", ev_prev_apply)
        ev_ld = p.dma("sync", sb.xbuf.rearrange("p (s f) -> p s f", s=16),
                      x_dram[:, ctt].rearrange("(s p) f -> p s f", p=128), "Wln")
        ev_stat = [None] * 16
        ev_sq = [None] * 16
        for s in range(16):
            sl2 = slice((s % 4) * 512, (s % 4) * 512 + 512)
            if s == 0:
                p.wait("vector", "Wln", ev_ld)
            if s >= 4:
                p.wait("vector", "PE", ev_stat[s - 4])
            ev_c = p.op("vector", "tensor_copy", sb.t_bf[:, sl2], sb.xbuf[:, s * 512:(s + 1) * 512], inc="DV")
            p.wait("scalar", "DV", ev_c)
            if s >= 4:
                p.wait("scalar", "PE", ev_stat[s - 4])
            ev_sq[s] = p.op("scalar", "activation", sb.t_bf2[:, sl2], sb.t_bf[:, sl2], AF.Square, inc="AC")
            p.wait("tensor", "AC", ev_sq[s])
            if s == 0:
                p.wait("tensor", "DV", ev_prev_rows)
            p.op("tensor", "matmul", sb.pR1[0:1, :], sb.ones_col[:, :], sb.t_bf[:, sl2],
                 start=(s == 0), stop=(s == 15))
            ev_stat[s] = p.op("tensor", "matmul", sb.pR2[0:1, :], sb.ones_col[:, :], sb.t_bf2[:, sl2],
                              start=(s == 0), stop=(s == 15), inc="PE")
        p.wait("vector", "PE", ev_stat[15])
        p.op("vector", "tensor_scalar", sb.r_mean[0:1, :], sb.pR1[0:1, :], 1.0 / D, None, ALU.mult)
        ev_prev_rows = p.op("vector", "tensor_scalar", sb.r_msq[0:1, :], sb.pR2[0:1, :], 1.0 / D, None,
                            ALU.mult, inc="DV")
        p.op("vector", "tensor_mul", sb.r_tmp[0:1, :], sb.r_mean[0:1, :], sb.r_mean[0:1, :])
        p.op("vector", "tensor_sub", sb.r_var[0:1, :], sb.r_msq[0:1, :], sb.r_tmp[0:1, :])
        p.op("vector", "tensor_scalar", sb.r_var2[0:1, :], sb.r_var[0:1, :], 1.0, LN_EPS, ALU.mult, ALU.add)
        p.op("vector", "reciprocal", sb.r_rv[0:1, :], sb.r_var2[0:1, :])
        for _ in range(8):
            p.op("vector", "engine_nop")
        ev_rv = p.op("vector", "engine_nop", inc="DV")
        p.wait("scalar", "DV", ev_rv)
        ev_r = p.op("scalar", "activation", sb.r_rstd[0:1, :], sb.r_rv[0:1, :], AF.Sqrt, inc="AC")
        p.wait("vector", "AC", ev_r)
        ev_nmr = p.op("vector", "scalar_tensor_tensor", sb.r_nmr[0:1, :], sb.r_mean[0:1, :], -1.0,
                      sb.r_rstd[0:1, :], ALU.mult, ALU.mult, inc="DV")
        p.wait("tensor", "DV", ev_nmr)
        p.wait("tensor", "DV", ev_prev_bcast_read)
        p.op("tensor", "matmul", sb.pE[:, :], sb.ones_rowf[:, :], sb.r_rstd[0:1, :], start=True, stop=True)
        ev_bc = p.op("tensor", "matmul", sb.pF[:, :], sb.ones_rowf[:, :], sb.r_nmr[0:1, :],
                     start=True, stop=True, inc="PE")
        ev_ap = [None] * 16
        ev_st = [None] * 16
        for s in range(16):
            sl2 = slice((s % 2) * 512, (s % 2) * 512 + 512)
            p.wait("vector", "PE", ev_bc)
            if s >= 2:
                p.wait("vector", "Gl0" if s % 2 == 0 else "Gl1", ev_st[s - 2])
            p.op("vector", "tensor_tensor", sb.t_f32[:, sl2], sb.xbuf[:, s * 512:(s + 1) * 512],
                 sb.pE[:, :], ALU.mult)
            ev_ap[s] = p.op("vector", "tensor_tensor", sb.t_lnout[:, sl2], sb.t_f32[:, sl2],
                            sb.pF[:, :], ALU.add, inc="DV")
            p.wait("gpsimd", "DV", ev_ap[s])
            ev_st[s] = p.dma("gpsimd", ln_dram[s * 128:(s + 1) * 128, ctt], sb.t_lnout[:, sl2],
                             "Gl0" if s % 2 == 0 else "Gl1")
        ev_prev_apply = ev_ap[15]
        ev_prev_bcast_read = ev_ap[15]
        p.wait("gpsimd", "Gl0", ev_st[14])
        p.wait("gpsimd", "Gl1", ev_st[15])
    p.barrier()



def emit_ffn(p, sb, ln_dram, x_in_dram, x_out_dram, dbg=None, cc_wait=None):
    NQ = 32
    if cc_wait:
        p.wait("sync", "CC", cc_wait)
        p.wait("gpsimd", "CC", cc_wait)
    for tt in range(NTT):
        ctt = slice(tt * 512, (tt + 1) * 512)
        p.gate()
        ev_lnall = p.dma("sync", sb.lnbuf.rearrange("p (s f) -> p s f", s=16),
                         ln_dram[:, ctt].rearrange("(s p) f -> p s f", p=128), "Wln")
        ev_xall = p.dma("scalar", sb.xbuf.rearrange("p (s f) -> p s f", s=16),
                        x_in_dram[:, ctt].rearrange("(s p) f -> p s f", p=128), "Xt")

        pair_mm = [None] * 64
        pair_silu = [None] * 64
        pair_h = [None] * 64
        pair_sq = [None] * 64
        pair_ssq = [None] * 64
        psum_g = [sb.pA, sb.pC, sb.pE]
        psum_u = [sb.pB, sb.pD, sb.pF]

        def emit_ssq(pr):
            p.wait("tensor", "AC", pair_sq[pr])
            pair_ssq[pr] = p.op("tensor", "matmul", sb.pR1[0:1, :], sb.ones_col[:, :],
                                sb.h2buf[:, (pr % 4) * 512:(pr % 4) * 512 + 512],
                                start=(pr == 0), stop=(pr == 63), inc="PE")

        for qd in range(NQ):
            buf = qd % 2
            gsl = sb.wstream[:, buf * 4096:buf * 4096 + 4096]
            usl = sb.wstream[:, 8192 + buf * 4096:8192 + buf * 4096 + 4096]
            if qd >= 2:
                p.wait("sync", "PE", pair_mm[2 * (qd - 2) + 1])
            gcol = 256 * qd
            ucol = HF + 256 * qd
            ev_wg = p.dma("sync", gsl.rearrange("p (c o) -> p c o", c=16),
                          sb.WguT_slab[gcol // 4096][:, gcol % 4096:gcol % 4096 + 256]
                          .rearrange("(c p) o -> p c o", p=128), "Wg%d" % buf)
            ev_wu = p.dma("sync", usl.rearrange("p (c o) -> p c o", c=16),
                          sb.WguT_slab[ucol // 4096][:, ucol % 4096:ucol % 4096 + 256]
                          .rearrange("(c p) o -> p c o", p=128), "Wu%d" % buf)
            if qd >= 1:
                emit_ssq(2 * (qd - 1))
                emit_ssq(2 * (qd - 1) + 1)
            p.wait("tensor", "Wg%d" % buf, ev_wg)
            p.wait("tensor", "Wu%d" % buf, ev_wu)
            if qd == 0:
                p.wait("tensor", "Wln", ev_lnall)
            for j in range(2):
                pr = 2 * qd + j
                sl3 = pr % 3
                if pr >= 3:
                    p.wait("tensor", "AC", pair_silu[pr - 3])
                    p.wait("tensor", "DV", pair_h[pr - 3])
                for c in range(16):
                    p.op("tensor", "matmul", psum_g[sl3][:, :],
                         gsl[:, c * 256 + j * 128:c * 256 + j * 128 + 128],
                         sb.lnbuf[:, c * 512:(c + 1) * 512], start=(c == 0), stop=(c == 15))
                for c in range(16):
                    pair_mm[pr] = p.op("tensor", "matmul", psum_u[sl3][:, :],
                                       usl[:, c * 256 + j * 128:c * 256 + j * 128 + 128],
                                       sb.lnbuf[:, c * 512:(c + 1) * 512], start=(c == 0), stop=(c == 15),
                                       inc="PE" if c == 15 else None)
            for j in range(2):
                pr = 2 * qd + j
                sl3 = pr % 3
                p.wait("scalar", "PE", pair_mm[pr])
                if pr >= 3:
                    p.wait("scalar", "DV", pair_h[pr - 3])
                pair_silu[pr] = p.op("scalar", "activation", sb.sgbuf[:, sl3 * 512:sl3 * 512 + 512],
                                     psum_g[sl3][:, :], AF.Silu, inc="AC")
                p.wait("vector", "AC", pair_silu[pr])
                pair_h[pr] = p.op("vector", "tensor_tensor", sb.hbuf[:, pr * 512:(pr + 1) * 512],
                                  sb.sgbuf[:, sl3 * 512:sl3 * 512 + 512], psum_u[sl3][:, :], ALU.mult, inc="DV")
                p.wait("scalar", "DV", pair_h[pr])
                if pr >= 4:
                    p.wait("scalar", "PE", pair_ssq[pr - 4])
                pair_sq[pr] = p.op("scalar", "activation", sb.h2buf[:, (pr % 4) * 512:(pr % 4) * 512 + 512],
                                   sb.hbuf[:, pr * 512:(pr + 1) * 512], AF.Square, inc="AC")
        emit_ssq(62)
        emit_ssq(63)

        p.wait("vector", "PE", pair_ssq[63])
        p.op("vector", "tensor_scalar", sb.r_ms[0:1, :], sb.pR1[0:1, :], 1.0 / HF, RMS_EPS, ALU.mult, ALU.add)
        p.op("vector", "reciprocal", sb.r_rv[0:1, :], sb.r_ms[0:1, :])
        for _ in range(8):
            p.op("vector", "engine_nop")
        ev_rv = p.op("vector", "engine_nop", inc="DV")
        p.wait("scalar", "DV", ev_rv)
        ev_rmsr = p.op("scalar", "activation", sb.r_rstd[0:1, :], sb.r_rv[0:1, :], AF.Sqrt, inc="AC")
        p.wait("tensor", "AC", ev_rmsr)
        p.wait("tensor", "DV", pair_h[62])
        ev_rmsbc = p.op("tensor", "matmul", sb.pE[:, :], sb.ones_rowf[:, :], sb.r_rstd[0:1, :],
                        start=True, stop=True, inc="PE")
        p.wait("vector", "PE", ev_rmsbc)
        p.op("vector", "tensor_copy", sb.rms_bs[:, :], sb.pE[:, :])
        if dbg is not None and tt == 0:
            p.gate()
            p.dma("gpsimd", dbg["h"].ap(), sb.hbuf[:, :], "Gd")
            p.dma("gpsimd", dbg["ln"].ap(), sb.lnbuf[:, :], "Gd")
            p.dma("gpsimd", dbg["rms"].ap(), sb.rms_bs[:, :], "Gd")
            p.wait("gpsimd", "Gd", p.ctr["Gd"])
            p.gate()

        pbank = [sb.pA, sb.pB, sb.pC, sb.pD]
        slab_pe = {}
        ev_evac = {}
        ev_store = {}
        p.wait("sync", "PE", pair_mm[63])
        idx = 0
        for s4 in range(4):
            for hc in range(64):
                wbuf = idx % 8
                slab = sb.wstream[:, wbuf * 512:wbuf * 512 + 512]
                if idx >= 8:
                    p.wait("sync", "PE", slab_pe[idx - 8])
                ev_w = p.dma("sync", slab, sb.WpT[hc * 128:(hc + 1) * 128, s4 * 512:s4 * 512 + 512],
                             "Wp%d" % wbuf)
                p.wait("tensor", "Wp%d" % wbuf, ev_w)
                if hc == 0 and s4 > 0:
                    p.wait("tensor", "DV", ev_evac[s4 * 4 - 1])
                if hc == 0 and s4 == 0:
                    p.wait("tensor", "AC", pair_silu[63])
                    p.wait("tensor", "DV", pair_h[63])
                for sl in range(4):
                    slab_pe[idx] = p.op("tensor", "matmul", pbank[sl][:, :],
                                        slab[:, sl * 128:sl * 128 + 128],
                                        sb.hbuf[:, hc * 512:(hc + 1) * 512],
                                        start=(hc == 0), stop=(hc == 63),
                                        inc="PE" if sl == 3 else None)
                idx += 1
            for sl in range(4):
                s = s4 * 4 + sl
                sl2 = slice((s % 2) * 512, (s % 2) * 512 + 512)
                p.wait("vector", "PE", slab_pe[idx - 1])
                if s == 0:
                    p.wait("vector", "Xt", ev_xall)
                if s >= 2:
                    p.wait("vector", "Go%d" % (s % 2), ev_store[s - 2])
                p.op("vector", "tensor_tensor", sb.t_f32[:, sl2], pbank[sl][:, :], sb.rms_bs[:, :], ALU.mult)
                ev_evac[s] = p.op("vector", "scalar_tensor_tensor", sb.xo[:, sl2], sb.t_f32[:, sl2], 0.5,
                                  sb.xbuf[:, s * 512:(s + 1) * 512], ALU.mult, ALU.add, inc="DV")
                p.wait("gpsimd", "DV", ev_evac[s])
                ev_store[s] = p.dma("gpsimd", x_out_dram[s * 128:(s + 1) * 128, ctt], sb.xo[:, sl2],
                                    "Go%d" % (s % 2))
        if dbg is not None and tt == 0:
            p.gate()
            p.dma("gpsimd", dbg["h2"].ap(), sb.hbuf[:, :], "Gd")
            p.wait("gpsimd", "Gd", p.ctr["Gd"])
            p.gate()
        p.wait("gpsimd", "Go0", ev_store[14])
        p.wait("gpsimd", "Go1", ev_store[15])
    p.barrier()



def emit_attn(p, sb):
    qk_psums = [sb.pA, sb.pB, sb.pC, sb.pD]
    pending_lazy = [None]
    ev_norm = [None, None]
    for hg in range(NH // 2):
        p.gate()
        base = hg * 256
        p.dma("sync", sb.wstream[:, 0:4096].rearrange("p (c o) -> p c o", c=16),
              sb.WqkvT[:, base:base + 256].rearrange("(c p) o -> p c o", p=128), "Wq")
        p.dma("sync", sb.wstream[:, 4096:8192].rearrange("p (c o) -> p c o", c=16),
              sb.WqkvT[:, D + base:D + base + 256].rearrange("(c p) o -> p c o", p=128), "Wq")
        ev_wv = p.dma("sync", sb.wstream[:, 8192:12288].rearrange("p (c o) -> p c o", c=16),
                      sb.WqkvT[:, 2 * D + base:2 * D + base + 256].rearrange("(c p) o -> p c o", p=128), "Wq")
        p.wait("tensor", "Wq", ev_wv)

        ev_chunk = {}
        ev_qkevac = None
        for tg in range(4):
            ev_mm = None
            for c in range(16):
                i = tg * 16 + c
                lbuf = i % 4
                lsl = sb.lnsl[:, lbuf * 512:lbuf * 512 + 512]
                if i >= 4:
                    p.wait("sync", "PE", ev_chunk[i - 4])
                ev_ln = p.dma("sync", lsl, sb.ln2T[c * 128:(c + 1) * 128, tg * 512:(tg + 1) * 512],
                              "Wl%d" % lbuf)
                p.wait("tensor", "Wl%d" % lbuf, ev_ln)
                if c == 0 and tg > 0:
                    p.wait("tensor", "DV", ev_qkevac)
                for sl in range(4):
                    wofs = [0, 4096, 0, 4096][sl] + c * 256 + (sl // 2) * 128
                    p.op("tensor", "matmul", qk_psums[sl][:, :], sb.wstream[:, wofs:wofs + 128],
                         lsl, start=(c == 0), stop=(c == 15))
                for ts in range(4):
                    vp = [sb.pE, sb.pF][ts // 2][:, (ts % 2) * 256:(ts % 2) * 256 + 256]
                    ev_mm = p.op("tensor", "matmul", vp,
                                 lsl[:, ts * 128:ts * 128 + 128],
                                 sb.wstream[:, 8192 + c * 256:8192 + c * 256 + 256],
                                 start=(c == 0 and ts % 2 == 0), stop=(c == 15),
                                 inc="PE" if ts == 3 else None)
                ev_chunk[i] = ev_mm
            p.wait("vector", "PE", ev_mm)
            for sl in range(4):
                p.op("vector", "tensor_copy",
                     sb.lnbuf[:, sl * 2048 + tg * 512:sl * 2048 + (tg + 1) * 512], qk_psums[sl][:, :])
            ev_qkevac = None
            for ts in range(4):
                tsub = tg * 4 + ts
                pp = [sb.pE, sb.pF][ts // 2]
                for j in range(2):
                    ev_qkevac = p.op("vector", "tensor_copy",
                                     sb.vbuf[:, j * 2048 + tsub * 128:j * 2048 + (tsub + 1) * 128],
                                     pp[:, (ts % 2) * 256 + j * 128:(ts % 2) * 256 + j * 128 + 128],
                                     inc="DV" if (ts == 3 and j == 1) else None)
        p.gate()

        for j in range(2):
            if pending_lazy[0] is not None:
                fn = pending_lazy[0]
                pending_lazy[0] = None
                fn()
            p.gate(engines=("scalar", "vector", "tensor"))
            h = 2 * hg + j
            Q = sb.lnbuf[:, (2 * j) * 2048:(2 * j) * 2048 + 2048]
            K = sb.lnbuf[:, (2 * j + 1) * 2048:(2 * j + 1) * 2048 + 2048]
            V = sb.vbuf[:, j * 2048:(j + 1) * 2048]
            for src, dst in ((Q, sb.r_qn2), (K, sb.r_kn2)):
                for i4 in range(4):
                    ssl = slice((i4 % 2) * 512, (i4 % 2) * 512 + 512)
                    if True:
                        pass
                    ev_sq = p.op("scalar", "activation", sb.pe_scr[:, ssl], src[:, i4 * 512:(i4 + 1) * 512],
                                 AF.Square, inc="AC")
                    p.wait("tensor", "AC", ev_sq)
                    ev_n = p.op("tensor", "matmul", sb.pR2[0:1, :], sb.ones_col[:, :], sb.pe_scr[:, ssl],
                                start=True, stop=True, inc="PE")
                    p.wait("vector", "PE", ev_n)
                    ev_cp = p.op("vector", "tensor_copy", dst[0:1, i4 * 512:(i4 + 1) * 512],
                                 sb.pR2[0:1, :], inc="DV")
                    p.wait("scalar", "DV", ev_cp)
            p.op("vector", "tensor_reduce", sb.r_sc1[0:1, 0:1], sb.r_qn2[0:1, :], AXX, ALU.max)
            p.op("vector", "tensor_reduce", sb.r_sc2[0:1, 0:1], sb.r_kn2[0:1, :], AXX, ALU.max)
            for _ in range(8):
                p.op("vector", "engine_nop")
            ev_red = p.op("vector", "engine_nop", inc="DV")
            p.wait("gpsimd", "DV", ev_red)
            p.op("gpsimd", "tensor_add", sb.r_sc3[0:1, 0:1], sb.r_sc1[0:1, 0:1], sb.r_sc2[0:1, 0:1])
            ev_nb = p.op("gpsimd", "tensor_scalar_mul", sb.r_sc3[0:1, 0:1], sb.r_sc3[0:1, 0:1],
                         -SM_SCALE / 2, inc="GP")
            p.wait("tensor", "GP", ev_nb)
            ev_nbc = p.op("tensor", "matmul", sb.pC[:, 0:1], sb.ones_rowf[:, :], sb.r_sc3[0:1, 0:1],
                          start=True, stop=True, inc="PE")
            p.wait("vector", "PE", ev_nbc)
            ev_nbcol = p.op("vector", "tensor_copy", sb.nbcol[:, :], sb.pC[:, 0:1], inc="DV")
            p.wait("scalar", "DV", ev_nbcol)
            p.wait("tensor", "DV", ev_nbcol)

            ev_exp_prev_qg = None
            ev_stage_prev = [None]
            for qg in range(4):
                nvis = 4 * qg + 4
                ev_exp = [None] * 64
                ev_pt = [None] * 64
                ev_lmm = [None] * 64
                banks = [sb.pA, sb.pB, sb.pD, sb.pF, sb.pE]

                def emit_pv(kc, last):
                    sem, val = ev_pt[kc]
                    p.wait("tensor", sem, val)
                    p.op("tensor", "matmul", sb.pC[:, :], V[:, kc * 128:(kc + 1) * 128],
                         sb.ptbuf[:, (kc % 4) * 512:(kc % 4) * 512 + 512],
                         start=(kc == 0), stop=last)
                    ev_lmm[kc] = p.op("tensor", "matmul", sb.pR1[0:1, :], sb.ones_col[:, :],
                                      sb.ptbuf[:, (kc % 4) * 512:(kc % 4) * 512 + 512],
                                      start=(kc == 0), stop=last, inc="PE")

                for kc in range(nvis):
                    pb = (kc + qg) % 5
                    psc = banks[pb]
                    if kc >= 5:
                        p.wait("tensor", "AC", ev_exp[kc - 5])
                    elif ev_exp_prev_qg:
                        p.wait("tensor", "AC", ev_exp_prev_qg)
                    if kc == 4 and qg >= 1 and ev_norm[(qg - 1) % 2] is not None:
                        p.wait("tensor", "DV", ev_norm[(qg - 1) % 2])
                    ev_sc = p.op("tensor", "matmul", psc[:, :], K[:, kc * 128:(kc + 1) * 128],
                                 Q[:, qg * 512:(qg + 1) * 512], start=True, stop=True, inc="PE")
                    p.wait("scalar", "PE", ev_sc)
                    diag = kc - 4 * qg
                    sl4 = kc % 4
                    if diag >= 0:
                        if kc >= 4:
                            p.wait("scalar", ev_pt[kc - 4][0], ev_pt[kc - 4][1])
                        ev_e = p.op("scalar", "activation", sb.pe_scr[:, sl4 * 512:sl4 * 512 + 512],
                                    psc[:, :], AF.Exp, bias=sb.nbcol[:, :], scale=SM_SCALE, inc="AC")
                        ev_exp[kc] = ev_e
                        p.wait("vector", "AC", ev_e)
                        if kc >= 4:
                            p.wait("vector", "PE", ev_lmm[kc - 4])
                        ev_pt[kc] = ("DV", p.op("vector", "tensor_tensor",
                                                sb.ptbuf[:, sl4 * 512:sl4 * 512 + 512],
                                                sb.pe_scr[:, sl4 * 512:sl4 * 512 + 512],
                                                sb.maskbuf[:, diag * 512:(diag + 1) * 512],
                                                ALU.mult, inc="DV"))
                    else:
                        if kc >= 4:
                            p.wait("scalar", "PE", ev_lmm[kc - 4])
                        ev_e = p.op("scalar", "activation", sb.ptbuf[:, sl4 * 512:sl4 * 512 + 512],
                                    psc[:, :], AF.Exp, bias=sb.nbcol[:, :], scale=SM_SCALE, inc="AC")
                        ev_exp[kc] = ev_e
                        ev_pt[kc] = ("AC", ev_e)
                    if kc >= 2 and pending_lazy[0] is not None:
                        fn = pending_lazy[0]
                        pending_lazy[0] = None
                        fn()
                    if kc >= 3:
                        if kc == 3 and ev_stage_prev[0] is not None:
                            p.wait("tensor", "DV", ev_stage_prev[0])
                        emit_pv(kc - 3, last=False)
                for tail in (3, 2, 1):
                    emit_pv(nvis - tail, last=(tail == 1))
                ev_l = ev_lmm[nvis - 1]
                ev_exp_prev_qg = ev_exp[nvis - 1]
                stage = sb.t_bf[:, (qg % 2) * 512:(qg % 2) * 512 + 512]
                p.wait("vector", "PE", ev_l)
                if ev_norm[qg % 2] is not None:
                    p.wait("vector", "DV", ev_norm[qg % 2])
                p.op("vector", "tensor_copy", stage, sb.pC[:, :])
                p.op("vector", "tensor_copy", sb.r_l[0:1, :], sb.pR1[0:1, :])
                ev_stage = p.op("vector", "engine_nop", inc="DV")
                ev_stage_prev[0] = ev_stage

                def lazy(qg=qg, h=h, stage=stage):
                    p.op("vector", "reciprocal", sb.r_rl[0:1, :], sb.r_l[0:1, :])
                    for _ in range(8):
                        p.op("vector", "engine_nop")
                    ev_rl = p.op("vector", "engine_nop", inc="DV")
                    p.wait("tensor", "DV", ev_rl)
                    rlbank = banks[qg % 5]
                    ev_rlbc = p.op("tensor", "matmul", rlbank[:, :], sb.ones_rowf[:, :],
                                   sb.r_rl[0:1, :], start=True, stop=True, inc="PE")
                    p.wait("vector", "PE", ev_rlbc)
                    p.op("vector", "tensor_copy", sb.rls[:, :], rlbank[:, :])
                    ev_norm[qg % 2] = p.op("vector", "tensor_tensor",
                                           sb.hbuf[:, h * 2048 + qg * 512:h * 2048 + (qg + 1) * 512],
                                           stage, sb.rls[:, :], ALU.mult, inc="DV")
                pending_lazy[0] = lazy

    if pending_lazy[0] is not None:
        fn = pending_lazy[0]
        pending_lazy[0] = None
        fn()
    pbank = [sb.pA, sb.pB, sb.pC, sb.pD]
    for tt in range(NTT):
        ctt = slice(tt * 512, (tt + 1) * 512)
        for s4 in range(4):
            p.gate()
            ev_x1 = p.dma("scalar", sb.xbuf[:, 0:2048].rearrange("p (s f) -> p s f", s=4),
                          sb.x1T[s4 * 512:(s4 + 1) * 512, ctt].rearrange("(s p) f -> p s f", p=128),
                          "Xt")
            slab_pe = {}
            for ac in range(16):
                wbuf = ac % 8
                slab = sb.wstream[:, wbuf * 512:wbuf * 512 + 512]
                if ac >= 8:
                    p.wait("sync", "PE", slab_pe[ac - 8])
                ev_w = p.dma("sync", slab, sb.WoT[ac * 128:(ac + 1) * 128, s4 * 512:s4 * 512 + 512],
                             "Wp%d" % wbuf)
                p.wait("tensor", "Wp%d" % wbuf, ev_w)
                for sl in range(4):
                    slab_pe[ac] = p.op("tensor", "matmul", pbank[sl][:, :],
                                       slab[:, sl * 128:sl * 128 + 128],
                                       sb.hbuf[:, ac * 2048 + tt * 512:ac * 2048 + (tt + 1) * 512],
                                       start=(ac == 0), stop=(ac == 15),
                                       inc="PE" if sl == 3 else None)
            ev_store = {}
            for sl in range(4):
                s = s4 * 4 + sl
                sl2 = slice((sl % 2) * 512, (sl % 2) * 512 + 512)
                p.wait("vector", "PE", slab_pe[15])
                if sl == 0:
                    p.wait("vector", "Xt", ev_x1)
                if sl >= 2:
                    p.wait("vector", "Go%d" % (sl % 2), ev_store[sl - 2])
                ev_e = p.op("vector", "tensor_tensor", sb.xo[:, sl2], pbank[sl][:, :],
                            sb.xbuf[:, sl * 512:(sl + 1) * 512], ALU.add, inc="DV")
                p.wait("gpsimd", "DV", ev_e)
                ev_store[sl] = p.dma("gpsimd", sb.x2T[s * 128:(s + 1) * 128, ctt], sb.xo[:, sl2],
                                     "Go%d" % (sl % 2))
            p.wait("gpsimd", "Go0", ev_store[2])
            p.wait("gpsimd", "Go1", ev_store[3])
    p.barrier()



def build_nc(debug_outputs=()):
    nc = bass.Bass("TRN2", target_bir_lowering=False, detect_race_conditions=False)

    class Ctx:
        pass

    sb = Ctx()
    sb.x0T = nc.dram_tensor("x0T", [D, S], F32, kind="ExternalInput")
    sb.wgu_sh = nc.dram_tensor("wgu_sh", [256, 2 * HF], F32, kind="ExternalInput")
    sb.wp_sh = nc.dram_tensor("wp_sh", [1024, D], F32, kind="ExternalInput")
    sb.wqkv_sh = nc.dram_tensor("wqkv_sh", [256, 3 * D], F32, kind="ExternalInput")
    sb.wo_sh = nc.dram_tensor("wo_sh", [256, D], F32, kind="ExternalInput")
    sb.yT = nc.dram_tensor("yT", [D, S], F32, kind="ExternalOutput")

    def internal(name, shape, dt):
        kind = "ExternalOutput" if name in debug_outputs else "Internal"
        return nc.dram_tensor(name, shape, dt, kind=kind)

    sb.q_wgu_slab = [internal(f"q_wgu_s{k}", [256, 4096], BF16) for k in range(4)]
    sb.WguT_slab = [internal(f"WguT_s{k}", [D, 4096], BF16) for k in range(4)]
    sb.q_wp_sh = internal("q_wp_sh", [1024, D], BF16)
    sb.b_wqkv_sh = internal("b_wqkv_sh", [256, 3 * D], BF16)
    sb.b_wo_sh = internal("b_wo_sh", [256, D], BF16)

    sb.WpT = internal("WpT", [HF, D], BF16)
    sb.WqkvT = internal("WqkvT", [D, 3 * D], BF16)
    sb.WoT = internal("WoT", [D, D], BF16)
    sb.dbg_h = internal("dbg_h", [128, 32768], BF16)
    sb.dbg_h2 = internal("dbg_h2", [128, 32768], BF16)
    sb.dbg_ln = internal("dbg_ln", [128, 8192], BF16)
    sb.dbg_rms = internal("dbg_rms", [128, 512], F32)
    sb.x1T = internal("x1T", [D, S], F32)
    sb.x2T = internal("x2T", [D, S], F32)
    sb.ln1T = internal("ln1T", [D, S], BF16)
    sb.ln2T = internal("ln2T", [D, S], BF16)
    sb.ln3T = internal("ln3T", [D, S], BF16)

    selb_np = np.zeros((128, 2), dtype=BF)
    selb_np[:64, 0] = 1
    selb_np[64:, 1] = 1
    selk2_np = np.zeros((2, 128), dtype=np.float32)
    selk2_np[0, :64] = 1
    selk2_np[1, 64:] = 1
    ones_col_np = np.ones((128, 1), dtype=BF)
    ones_rowf_np = np.ones((1, 128), dtype=np.float32)
    masks_np = np.zeros((4 * 128, 512), dtype=BF)
    for jj in range(4):
        for pp in range(128):
            masks_np[jj * 128 + pp, 128 * jj + pp:] = 1
    C_SELB = nc.inline_tensor(selb_np, "c_selb")
    C_SELK2 = nc.inline_tensor(selk2_np, "c_selk2")
    C_ONESC = nc.inline_tensor(ones_col_np, "c_onesc")
    C_ONESRF = nc.inline_tensor(ones_rowf_np, "c_onesrf")
    C_MASKS = nc.inline_tensor(masks_np, "c_masks")

    with ExitStack() as stack:
        def sbuf(name, shape, dt):
            return stack.enter_context(nc.sbuf_tensor(name, shape, dt))

        def psum(name, shape, dt):
            return stack.enter_context(nc.psum_tensor(name, shape, dt))

        sb.lnbuf = sbuf("lnbuf", [128, 8192], BF16)
        sb.hbuf = sbuf("hbuf", [128, 32768], BF16)
        sb.xbuf = sbuf("xbuf", [128, 8192], F32)
        sb.wstream = sbuf("wstream", [128, 16384], BF16)
        sb.vbuf = sbuf("vbuf", [128, 4096], BF16)
        sb.sgbuf = sbuf("sgbuf", [128, 1536], F32)
        sb.h2buf = None
        sb.t_bf = sbuf("t_bf", [128, 2048], BF16)
        sb.t_bf2 = sbuf("t_bf2", [128, 2048], BF16)
        sb.t_f32 = sbuf("t_f32", [128, 1024], F32)
        sb.t_lnout = None
        sb.xo = sbuf("xo", [128, 1024], F32)
        sb.rms_bs = sbuf("rms_bs", [128, 512], F32)
        sb.rls = sbuf("rls", [128, 512], F32)
        sb.ptbuf = sbuf("ptbuf", [128, 2048], BF16)
        sb.pe_scr = sbuf("pe_scr", [128, 2048], BF16)
        sb.h2buf = sb.ptbuf
        sb.t_lnout = sb.pe_scr[:, 1024:2048]
        sb.lnsl = sbuf("lnsl", [128, 2048], BF16)
        sb.maskbuf = sbuf("maskbuf", [128, 2048], BF16)
        sb.nbcol = sbuf("nbcol", [128, 1], F32)
        sb.q_rvrow = sbuf("q_rvrow", [2, 512], F32)
        _qsb = sbuf("q_scaleb", [2, 1024], BF16)
        sb.q_scaleb = [_qsb[:, 0:512], _qsb[:, 512:1024]]
        sb.q_scalef = [sb.sgbuf[0:2, 0:512], sb.sgbuf[0:2, 512:1024]]
        sb.q_thr = [sb.sgbuf[0:2, 1024:1536], sb.q_rvrow[0:2, 0:512]]
        sb.r_mean = sbuf("r_mean", [1, 512], F32)
        sb.r_msq = sbuf("r_msq", [1, 512], F32)
        sb.r_rv = sbuf("r_rv", [1, 512], F32)
        sb.r_rstd = sbuf("r_rstd", [1, 512], F32)
        sb.r_tmp = sb.sgbuf[0:1, 1024:1536]
        sb.r_var = sb.r_rstd
        sb.r_nmr = sb.r_rv
        sb.r_var2 = sb.r_tmp
        sb.r_ms = sb.r_var
        sb.r_l = sb.r_mean
        sb.r_rl = sb.r_msq
        sb.q_w32 = [sb.t_f32[:, 0:512], sb.t_f32[:, 512:1024]]
        sb.q_wb = [sb.t_bf[:, 0:512], sb.t_bf[:, 1024:1536]]
        sb.q_d = [sb.t_bf[:, 512:1024], sb.t_bf[:, 1536:2048]]
        sb.q_aw = [sb.t_bf2[:, 0:512], sb.t_bf2[:, 512:1024]]
        sb.q_sgn = [sb.t_lnout[:, 0:512], sb.t_lnout[:, 512:1024]]
        sb.q_m = [sb.ptbuf[:, 0:512], sb.ptbuf[:, 1024:1536]]
        sb.q_out = [sb.ptbuf[:, 512:1024], sb.ptbuf[:, 1536:2048]]
        sb.q_q = [sb.pe_scr[:, 0:512], sb.pe_scr[:, 512:1024]]
        sb.q_what = [sb.lnsl[:, 0:512], sb.lnsl[:, 512:1024]]
        sb.r_qn2 = sb.xbuf[0:1, 0:2048]
        sb.r_kn2 = sb.xbuf[0:1, 2048:4096]
        sb.r_sc1 = sbuf("r_sc1", [1, 1], F32)
        sb.r_sc2 = sbuf("r_sc2", [1, 1], F32)
        sb.r_sc3 = sbuf("r_sc3", [1, 1], F32)
        sb.selb = sbuf("selb", [128, 2], BF16)
        sb.selk2 = sbuf("selk2", [2, 128], F32)
        sb.ones_col = sbuf("ones_col", [128, 1], BF16)
        sb.ones_rowf = sbuf("ones_rowf", [1, 128], F32)
        sb.pA = psum("pA", [128, 512], F32)
        sb.pB = psum("pB", [128, 512], F32)
        sb.pC = psum("pC", [128, 512], F32)
        sb.pD = psum("pD", [128, 512], F32)
        sb.pE = psum("pE", [128, 512], F32)
        sb.pF = psum("pF", [128, 512], F32)
        sb.pR1 = psum("pR1", [2, 512], F32)
        sb.pR2 = psum("pR2", [2, 512], F32)

        sems = {n: stack.enter_context(nc.semaphore("s" + n)) for n in SEMNAMES}
        p = P(sems)

        for dst, src in ((sb.selb, C_SELB), (sb.selk2, C_SELK2),
                         (sb.ones_col, C_ONESC), (sb.ones_rowf, C_ONESRF)):
            p.dma("sync", dst[:, :], src[:, :], "Wc")
        for jj in range(4):
            p.dma("sync", sb.maskbuf[:, jj * 512:(jj + 1) * 512], C_MASKS[jj * 128:(jj + 1) * 128, :], "Wc")
        p.gate()

        cc_target = emit_prep(p, sb)
        emit_ln_pass(p, sb, sb.x0T, sb.ln1T)
        dbg = ({"h": sb.dbg_h, "h2": sb.dbg_h2, "ln": sb.dbg_ln, "rms": sb.dbg_rms}
               if "dbg_h" in debug_outputs else None)
        emit_ffn(p, sb, sb.ln1T, sb.x0T, sb.x1T, dbg=dbg, cc_wait=cc_target)
        emit_ln_pass(p, sb, sb.x1T, sb.ln2T)
        emit_attn(p, sb)
        emit_ln_pass(p, sb, sb.x2T, sb.ln3T)
        emit_ffn(p, sb, sb.ln3T, sb.x2T, sb.yT)

        with nc.Block() as block:
            for ename in ENGINES:
                def make_body(ename):
                    def body(eng):
                        for f in p.ops[ename]:
                            f(eng)
                    return body
                getattr(block, ename)(make_body(ename))

    return nc


def kernel(L0_attn_hidden_states, L0_attn_w_qkv, L0_attn_w_o, L0_ffn_w_gate_up, L0_ffn_w_proj):
    hs = np.asarray(L0_attn_hidden_states, dtype=np.float32)
    w_qkv = np.asarray(L0_attn_w_qkv, dtype=np.float32)
    w_o = np.asarray(L0_attn_w_o, dtype=np.float32)
    w_gu = np.asarray(L0_ffn_w_gate_up, dtype=np.float32)
    w_p = np.asarray(L0_ffn_w_proj, dtype=np.float32)

    nc = build_nc()
    in_maps = []
    for c in range(NC):
        in_maps.append({
            "x0T": np.ascontiguousarray(hs[c].T),
            "wgu_sh": np.ascontiguousarray(w_gu[:, c * 256:(c + 1) * 256].T),
            "wp_sh": np.ascontiguousarray(w_p[:, c * 1024:(c + 1) * 1024].T),
            "wqkv_sh": np.ascontiguousarray(w_qkv[:, c * 256:(c + 1) * 256].T),
            "wo_sh": np.ascontiguousarray(w_o[:, c * 256:(c + 1) * 256].T),
        })
    res = run_bass_kernel_spmd(nc, in_maps, core_ids=list(range(NC)))
    return np.stack([np.ascontiguousarray(r["yT"].T) for r in res.results])


# version 10 (final)
# speedup vs baseline: 1.0322x; 1.0126x over previous
import math
import numpy as np
import ml_dtypes
from contextlib import ExitStack

import concourse.bass as bass
import concourse.mybir as mybir
from concourse.bass_utils import run_bass_kernel_spmd

F32 = mybir.dt.float32
BF16 = mybir.dt.bfloat16
AF = mybir.ActivationFunctionType
ALU = mybir.AluOpType
AXX = mybir.AxisListType.X

BF = ml_dtypes.bfloat16

D = 2048
S = 2048
NH = 16
HD = 128
HF = 8192
NC = 8
TT = 512
NTT = S // TT
SM_SCALE = 1.0 / math.sqrt(HD)
THR_C = 0.5 * (1.0 + 2.0 ** -9)
LN_EPS = 1e-5
RMS_EPS = float(np.finfo(np.float32).eps)

ENGINES = ("sync", "vector", "scalar", "tensor", "gpsimd")
SEMNAMES = (["Wg0", "Wg1", "Wu0", "Wu1"] + [f"Wp{i}" for i in range(8)]
            + ["Wl0", "Wl1", "Wl2", "Wl3", "Wq", "Wln", "Wz", "Wz2", "Wc", "Xt",
               "Go0", "Go1", "Gl0", "Gl1", "Gq", "Gq2", "Gd",
               "PE", "DV", "AC", "GP", "CC", "B"])
DMA_SEMS = tuple(n for n in SEMNAMES if n[0] in "WXG" and n != "GP")


class P:

    def __init__(self, sems):
        self.ops = {e: [] for e in ENGINES}
        self.sems = sems
        self.ctr = {name: 0 for name in sems}

    def op(self, engine, method, *args, inc=None, inc_n=1, **kwargs):
        val = None
        if inc is not None:
            self.ctr[inc] += inc_n
            val = self.ctr[inc]
            sem = self.sems[inc]

            def fn(eng, method=method, args=args, kwargs=kwargs, sem=sem, n=inc_n):
                getattr(eng, method)(*args, **kwargs).then_inc(sem, n)
        else:
            def fn(eng, method=method, args=args, kwargs=kwargs):
                getattr(eng, method)(*args, **kwargs)
        self.ops[engine].append(fn)
        return val

    def dma(self, engine, out, in_, sem):
        return self.op(engine, "dma_start", inc=sem, inc_n=16, out=out, in_=in_)

    def wait(self, engine, semname, val):
        if not val:
            return
        sem = self.sems[semname]

        def fn(eng, sem=sem, val=val):
            eng.wait_ge(sem, val)
        self.ops[engine].append(fn)

    def gate(self, engines=ENGINES, sems=None):
        if sems is None:
            sems = [n for n in SEMNAMES if n not in ("B", "CC")]
        snap = {k: self.ctr[k] for k in sems}
        own = {"sync": None, "vector": "DV", "scalar": "AC", "tensor": "PE", "gpsimd": "GP"}
        for e in engines:
            for k, v in snap.items():
                if v and own.get(e) != k:
                    self.wait(e, k, v)

    def barrier(self):
        self.ctr["B"] += len(ENGINES)
        target = self.ctr["B"]
        sem = self.sems["B"]
        for e in ENGINES:
            def fn(eng, sem=sem, target=target):
                eng.drain()
                eng.nop().then_inc(sem, 1)
                eng.wait_ge(sem, target)
            self.ops[e].append(fn)



def emit_quant_shard(p, sb, src, dst, R, C):
    n_r, n_c = R // 128, C // 512
    n = n_r * n_c
    pr_asum = [sb.pR1, sb.pR2]
    pr_thr = [sb.pE, sb.pA]
    pr_scale = [sb.pF, sb.pB]
    ev = {}
    rcsl = lambda j: (slice((j // n_c) * 128, (j // n_c) * 128 + 128),
                      slice((j % n_c) * 512, (j % n_c) * 512 + 512), j % 2)
    for i in range(n + 2):
        if i < n:
            rr, cc, sl = rcsl(i)
            ldsem = ("Wz", "Wz2")[sl]
            p.wait("sync", "DV", ev.get(("cast", i - 2)))
            ev[("ld", i)] = p.dma("sync", sb.q_w32[sl], src[rr, cc], ldsem)
        j = i - 2
        if j >= 0:
            sl = j % 2
            stsem = ("Gq", "Gq2")[sl]
            p.wait("vector", "PE", ev.get(("bcast", j)))
            p.wait("vector", stsem, ev.get(("store", j - 2)))
            p.op("vector", "tensor_tensor", sb.q_m[sl], sb.q_aw[sl], pr_thr[sl][:, :], ALU.is_gt)
            p.op("vector", "tensor_mul", sb.q_q[sl], sb.q_m[sl], sb.q_sgn[sl])
            p.op("vector", "tensor_tensor", sb.q_what[sl], sb.q_q[sl], pr_scale[sl][:, :], ALU.mult)
            p.op("vector", "tensor_sub", sb.q_d[sl], sb.q_what[sl], sb.q_wb[sl])
            ev[("out", j)] = p.op("vector", "tensor_add", sb.q_out[sl], sb.q_wb[sl], sb.q_d[sl], inc="DV")
        j = i - 1
        if 0 <= j < n:
            sl = j % 2
            p.wait("vector", "AC", ev.get(("scaleb", j)))
            p.wait("vector", "PE", ev.get(("bcast", j - 2)))
            p.op("vector", "tensor_scalar_max", sb.q_scalef[sl], sb.q_scaleb[sl], 1e-8)
            ev[("thr", j)] = p.op("vector", "tensor_scalar_mul", sb.q_thr[sl], sb.q_scalef[sl], THR_C, inc="DV")
        if i < n:
            sl = i % 2
            ldsem = ("Wz", "Wz2")[sl]
            p.wait("vector", ldsem, ev.get(("ld", i)))
            p.wait("vector", "AC", ev.get(("absgn", i - 2)))
            ev[("cast", i)] = p.op("vector", "tensor_copy", sb.q_wb[sl], sb.q_w32[sl], inc="DV")
        if i < n:
            sl = i % 2
            p.wait("scalar", "DV", ev.get(("cast", i)))
            p.wait("scalar", "PE", ev.get(("asum", i - 2)))
            p.wait("scalar", "DV", ev.get(("out", i - 2)))
            p.op("scalar", "activation", sb.q_aw[sl], sb.q_wb[sl], AF.Abs)
            ev[("absgn", i)] = p.op("scalar", "activation", sb.q_sgn[sl], sb.q_wb[sl], AF.Sign, inc="AC")
        if i < n:
            sl = i % 2
            p.wait("tensor", "AC", ev.get(("absgn", i)))
            p.wait("tensor", "AC", ev.get(("scaleb", i - 2)))
            ev[("asum", i)] = p.op("tensor", "matmul", pr_asum[sl][:, :], sb.selb[:, :], sb.q_aw[sl],
                                   start=True, stop=True, inc="PE")
            p.wait("scalar", "PE", ev.get(("asum", i)))
            p.wait("scalar", "DV", ev.get(("thr", i - 2)))
            ev[("scaleb", i)] = p.op("scalar", "activation", sb.q_scaleb[sl], pr_asum[sl][:, :], AF.Copy,
                                     bias=0.0, scale=1.0 / 64, inc="AC")
        j = i - 1
        if 0 <= j < n:
            sl = j % 2
            p.wait("tensor", "DV", ev.get(("thr", j)))
            p.wait("tensor", "DV", ev.get(("out", j - 2)))
            p.op("tensor", "matmul", pr_thr[sl][:, :], sb.selk2[:, :], sb.q_thr[sl], start=True, stop=True)
            ev[("bcast", j)] = p.op("tensor", "matmul", pr_scale[sl][:, :], sb.selk2[:, :], sb.q_scalef[sl],
                                    start=True, stop=True, inc="PE")
        j = i - 2
        if j >= 0:
            rr, cc, sl = rcsl(j)
            stsem = ("Gq", "Gq2")[sl]
            p.wait("gpsimd", "DV", ev.get(("out", j)))
            ev[("store", j)] = p.dma("gpsimd", dst[rr, cc], sb.q_out[sl], stsem)
    last0, last1 = (n - 2, n - 1) if n % 2 == 0 else (n - 1, n - 2)
    p.wait("gpsimd", "Gq", ev.get(("store", last0)))
    p.wait("gpsimd", "Gq2", ev.get(("store", last1)))
    return None


def emit_cast_shard(p, sb, src, dst, R, C):
    n_r, n_c = R // 128, C // 512
    evc = {}
    evs = {}
    ev_store = None
    for i in range(n_r * n_c):
        r, c = divmod(i, n_c)
        sl = i % 2
        ldsem = ("Wz", "Wz2")[sl]
        stsem = ("Gq", "Gq2")[sl]
        rr = slice(r * 128, (r + 1) * 128)
        cc = slice(c * 512, (c + 1) * 512)
        p.wait("sync", "DV", evc.get(sl))
        ev_ld = p.dma("sync", sb.q_w32[sl], src[rr, cc], ldsem)
        p.wait("vector", ldsem, ev_ld)
        p.wait("vector", stsem, evs.get(sl))
        evc[sl] = p.op("vector", "tensor_copy", sb.q_wb[sl], sb.q_w32[sl], inc="DV")
        p.wait("gpsimd", "DV", evc[sl])
        evs[sl] = p.dma("gpsimd", dst[rr, cc], sb.q_wb[sl], stsem)
    p.wait("gpsimd", "Gq", evs.get(0))
    p.wait("gpsimd", "Gq2", evs.get(1))
    return None


def emit_prep(p, sb):
    for fn, srcw, dstq, dstg, R, C in (
            (emit_cast_shard, sb.wo_sh, sb.b_wo_sh, sb.WoT, 256, D),
            (emit_cast_shard, sb.wqkv_sh, sb.b_wqkv_sh, sb.WqkvT, 256, 3 * D),
            (emit_quant_shard, sb.wp_sh, sb.q_wp_sh, sb.WpT, 1024, D)):
        fn(p, sb, srcw, dstq, R, C)
        p.op("gpsimd", "collective_compute", "AllGather", ALU.bypass,
             [list(range(NC))], [dstq.ap()], [dstg.ap()], inc="CC")
        p.gate(engines=("sync", "vector", "scalar", "tensor"))
    for k in range(4):
        emit_quant_shard(p, sb, sb.wgu_sh[:, k * 4096:(k + 1) * 4096],
                         sb.q_wgu_slab[k], 256, 4096)
        p.op("gpsimd", "collective_compute", "AllGather", ALU.bypass,
             [list(range(NC))], [sb.q_wgu_slab[k].ap()], [sb.WguT_slab[k].ap()], inc="CC")
        p.gate(engines=("sync", "vector", "scalar", "tensor"))
    return p.ctr["CC"]



def emit_ln_pass(p, sb, x_dram, ln_dram):
    ev_prev_apply = None
    ev_prev_rows = None
    ev_prev_bcast_read = None
    for tt in range(NTT):
        ctt = slice(tt * 512, (tt + 1) * 512)
        if tt == 0:
            p.gate()
        p.wait("sync", "DV", ev_prev_apply)
        ev_ld = p.dma("sync", sb.xbuf.rearrange("p (s f) -> p s f", s=16),
                      x_dram[:, ctt].rearrange("(s p) f -> p s f", p=128), "Wln")
        ev_stat = [None] * 16
        ev_sq = [None] * 16
        for s in range(16):
            sl2 = slice((s % 4) * 512, (s % 4) * 512 + 512)
            if s == 0:
                p.wait("vector", "Wln", ev_ld)
            if s >= 4:
                p.wait("vector", "PE", ev_stat[s - 4])
            ev_c = p.op("vector", "tensor_copy", sb.t_bf[:, sl2], sb.xbuf[:, s * 512:(s + 1) * 512], inc="DV")
            p.wait("scalar", "DV", ev_c)
            if s >= 4:
                p.wait("scalar", "PE", ev_stat[s - 4])
            ev_sq[s] = p.op("scalar", "activation", sb.t_bf2[:, sl2], sb.t_bf[:, sl2], AF.Square, inc="AC")
            p.wait("tensor", "AC", ev_sq[s])
            if s == 0:
                p.wait("tensor", "DV", ev_prev_rows)
            p.op("tensor", "matmul", sb.pR1[0:1, :], sb.ones_col[:, :], sb.t_bf[:, sl2],
                 start=(s == 0), stop=(s == 15))
            ev_stat[s] = p.op("tensor", "matmul", sb.pR2[0:1, :], sb.ones_col[:, :], sb.t_bf2[:, sl2],
                              start=(s == 0), stop=(s == 15), inc="PE")
        p.wait("vector", "PE", ev_stat[15])
        p.op("vector", "tensor_scalar", sb.r_mean[0:1, :], sb.pR1[0:1, :], 1.0 / D, None, ALU.mult)
        ev_prev_rows = p.op("vector", "tensor_scalar", sb.r_msq[0:1, :], sb.pR2[0:1, :], 1.0 / D, None,
                            ALU.mult, inc="DV")
        p.op("vector", "tensor_mul", sb.r_tmp[0:1, :], sb.r_mean[0:1, :], sb.r_mean[0:1, :])
        p.op("vector", "tensor_sub", sb.r_var[0:1, :], sb.r_msq[0:1, :], sb.r_tmp[0:1, :])
        p.op("vector", "tensor_scalar", sb.r_var2[0:1, :], sb.r_var[0:1, :], 1.0, LN_EPS, ALU.mult, ALU.add)
        p.op("vector", "reciprocal", sb.r_rv[0:1, :], sb.r_var2[0:1, :])
        for _ in range(8):
            p.op("vector", "engine_nop")
        ev_rv = p.op("vector", "engine_nop", inc="DV")
        p.wait("scalar", "DV", ev_rv)
        ev_r = p.op("scalar", "activation", sb.r_rstd[0:1, :], sb.r_rv[0:1, :], AF.Sqrt, inc="AC")
        p.wait("vector", "AC", ev_r)
        ev_nmr = p.op("vector", "scalar_tensor_tensor", sb.r_nmr[0:1, :], sb.r_mean[0:1, :], -1.0,
                      sb.r_rstd[0:1, :], ALU.mult, ALU.mult, inc="DV")
        p.wait("tensor", "DV", ev_nmr)
        p.wait("tensor", "DV", ev_prev_bcast_read)
        p.op("tensor", "matmul", sb.pE[:, :], sb.ones_rowf[:, :], sb.r_rstd[0:1, :], start=True, stop=True)
        ev_bc = p.op("tensor", "matmul", sb.pF[:, :], sb.ones_rowf[:, :], sb.r_nmr[0:1, :],
                     start=True, stop=True, inc="PE")
        ev_ap = [None] * 16
        ev_st = [None] * 16
        for s in range(16):
            sl2 = slice((s % 2) * 512, (s % 2) * 512 + 512)
            p.wait("vector", "PE", ev_bc)
            if s >= 2:
                p.wait("vector", "Gl0" if s % 2 == 0 else "Gl1", ev_st[s - 2])
            p.op("vector", "tensor_tensor", sb.t_f32[:, sl2], sb.xbuf[:, s * 512:(s + 1) * 512],
                 sb.pE[:, :], ALU.mult)
            ev_ap[s] = p.op("vector", "tensor_tensor", sb.t_lnout[:, sl2], sb.t_f32[:, sl2],
                            sb.pF[:, :], ALU.add, inc="DV")
            p.wait("gpsimd", "DV", ev_ap[s])
            ev_st[s] = p.dma("gpsimd", ln_dram[s * 128:(s + 1) * 128, ctt], sb.t_lnout[:, sl2],
                             "Gl0" if s % 2 == 0 else "Gl1")
        ev_prev_apply = ev_ap[15]
        ev_prev_bcast_read = ev_ap[15]
        p.wait("gpsimd", "Gl0", ev_st[14])
        p.wait("gpsimd", "Gl1", ev_st[15])
    p.barrier()



def emit_ffn(p, sb, ln_dram, x_in_dram, x_out_dram, dbg=None, cc_wait=None):
    NQ = 32
    if cc_wait:
        p.wait("sync", "CC", cc_wait)
        p.wait("gpsimd", "CC", cc_wait)
    for tt in range(NTT):
        ctt = slice(tt * 512, (tt + 1) * 512)
        p.gate()
        ev_lnall = p.dma("sync", sb.lnbuf.rearrange("p (s f) -> p s f", s=16),
                         ln_dram[:, ctt].rearrange("(s p) f -> p s f", p=128), "Wln")
        ev_xall = p.dma("scalar", sb.xbuf.rearrange("p (s f) -> p s f", s=16),
                        x_in_dram[:, ctt].rearrange("(s p) f -> p s f", p=128), "Xt")

        pair_mm = [None] * 64
        pair_silu = [None] * 64
        pair_h = [None] * 64
        pair_sq = [None] * 64
        pair_ssq = [None] * 64
        psum_g = [sb.pA, sb.pC, sb.pE]
        psum_u = [sb.pB, sb.pD, sb.pF]

        def emit_ssq(pr):
            p.wait("tensor", "AC", pair_sq[pr])
            pair_ssq[pr] = p.op("tensor", "matmul", sb.pR1[0:1, :], sb.ones_col[:, :],
                                sb.h2buf[:, (pr % 4) * 512:(pr % 4) * 512 + 512],
                                start=(pr == 0), stop=(pr == 63), inc="PE")

        for qd in range(NQ):
            buf = qd % 2
            gsl = sb.wstream[:, buf * 4096:buf * 4096 + 4096]
            usl = sb.wstream[:, 8192 + buf * 4096:8192 + buf * 4096 + 4096]
            if qd >= 2:
                p.wait("sync", "PE", pair_mm[2 * (qd - 2) + 1])
            gcol = 256 * qd
            ucol = HF + 256 * qd
            ev_wg = p.dma("sync", gsl.rearrange("p (c o) -> p c o", c=16),
                          sb.WguT_slab[gcol // 4096][:, gcol % 4096:gcol % 4096 + 256]
                          .rearrange("(c p) o -> p c o", p=128), "Wg%d" % buf)
            ev_wu = p.dma("sync", usl.rearrange("p (c o) -> p c o", c=16),
                          sb.WguT_slab[ucol // 4096][:, ucol % 4096:ucol % 4096 + 256]
                          .rearrange("(c p) o -> p c o", p=128), "Wu%d" % buf)
            if qd >= 1:
                emit_ssq(2 * (qd - 1))
                emit_ssq(2 * (qd - 1) + 1)
            p.wait("tensor", "Wg%d" % buf, ev_wg)
            p.wait("tensor", "Wu%d" % buf, ev_wu)
            if qd == 0:
                p.wait("tensor", "Wln", ev_lnall)
            for j in range(2):
                pr = 2 * qd + j
                sl3 = pr % 3
                if pr >= 3:
                    p.wait("tensor", "AC", pair_silu[pr - 3])
                    p.wait("tensor", "DV", pair_h[pr - 3])
                for c in range(16):
                    p.op("tensor", "matmul", psum_g[sl3][:, :],
                         gsl[:, c * 256 + j * 128:c * 256 + j * 128 + 128],
                         sb.lnbuf[:, c * 512:(c + 1) * 512], start=(c == 0), stop=(c == 15))
                for c in range(16):
                    pair_mm[pr] = p.op("tensor", "matmul", psum_u[sl3][:, :],
                                       usl[:, c * 256 + j * 128:c * 256 + j * 128 + 128],
                                       sb.lnbuf[:, c * 512:(c + 1) * 512], start=(c == 0), stop=(c == 15),
                                       inc="PE" if c == 15 else None)
            for j in range(2):
                pr = 2 * qd + j
                sl3 = pr % 3
                p.wait("scalar", "PE", pair_mm[pr])
                if pr >= 3:
                    p.wait("scalar", "DV", pair_h[pr - 3])
                pair_silu[pr] = p.op("scalar", "activation", sb.sgbuf[:, sl3 * 512:sl3 * 512 + 512],
                                     psum_g[sl3][:, :], AF.Silu, inc="AC")
                p.wait("vector", "AC", pair_silu[pr])
                pair_h[pr] = p.op("vector", "tensor_tensor", sb.hbuf[:, pr * 512:(pr + 1) * 512],
                                  sb.sgbuf[:, sl3 * 512:sl3 * 512 + 512], psum_u[sl3][:, :], ALU.mult, inc="DV")
                p.wait("scalar", "DV", pair_h[pr])
                if pr >= 4:
                    p.wait("scalar", "PE", pair_ssq[pr - 4])
                pair_sq[pr] = p.op("scalar", "activation", sb.h2buf[:, (pr % 4) * 512:(pr % 4) * 512 + 512],
                                   sb.hbuf[:, pr * 512:(pr + 1) * 512], AF.Square, inc="AC")
        emit_ssq(62)
        emit_ssq(63)

        p.wait("vector", "PE", pair_ssq[63])
        p.op("vector", "tensor_scalar", sb.r_ms[0:1, :], sb.pR1[0:1, :], 1.0 / HF, RMS_EPS, ALU.mult, ALU.add)
        p.op("vector", "reciprocal", sb.r_rv[0:1, :], sb.r_ms[0:1, :])
        for _ in range(8):
            p.op("vector", "engine_nop")
        ev_rv = p.op("vector", "engine_nop", inc="DV")
        p.wait("scalar", "DV", ev_rv)
        ev_rmsr = p.op("scalar", "activation", sb.r_rstd[0:1, :], sb.r_rv[0:1, :], AF.Sqrt, inc="AC")
        p.wait("tensor", "AC", ev_rmsr)
        p.wait("tensor", "DV", pair_h[62])
        ev_rmsbc = p.op("tensor", "matmul", sb.pE[:, :], sb.ones_rowf[:, :], sb.r_rstd[0:1, :],
                        start=True, stop=True, inc="PE")
        p.wait("vector", "PE", ev_rmsbc)
        p.op("vector", "tensor_copy", sb.rms_bs[:, :], sb.pE[:, :])
        if dbg is not None and tt == 0:
            p.gate()
            p.dma("gpsimd", dbg["h"].ap(), sb.hbuf[:, :], "Gd")
            p.dma("gpsimd", dbg["ln"].ap(), sb.lnbuf[:, :], "Gd")
            p.dma("gpsimd", dbg["rms"].ap(), sb.rms_bs[:, :], "Gd")
            p.wait("gpsimd", "Gd", p.ctr["Gd"])
            p.gate()

        pbank = [sb.pA, sb.pB, sb.pC, sb.pD]
        slab_pe = {}
        ev_evac = {}
        ev_store = {}
        p.wait("sync", "PE", pair_mm[63])
        idx = 0
        for s4 in range(4):
            for hc in range(64):
                wbuf = idx % 8
                slab = sb.wstream[:, wbuf * 512:wbuf * 512 + 512]
                if idx >= 8:
                    p.wait("sync", "PE", slab_pe[idx - 8])
                ev_w = p.dma("sync", slab, sb.WpT[hc * 128:(hc + 1) * 128, s4 * 512:s4 * 512 + 512],
                             "Wp%d" % wbuf)
                p.wait("tensor", "Wp%d" % wbuf, ev_w)
                if hc == 0 and s4 > 0:
                    p.wait("tensor", "DV", ev_evac[s4 * 4 - 1])
                if hc == 0 and s4 == 0:
                    p.wait("tensor", "AC", pair_silu[63])
                    p.wait("tensor", "DV", pair_h[63])
                for sl in range(4):
                    slab_pe[idx] = p.op("tensor", "matmul", pbank[sl][:, :],
                                        slab[:, sl * 128:sl * 128 + 128],
                                        sb.hbuf[:, hc * 512:(hc + 1) * 512],
                                        start=(hc == 0), stop=(hc == 63),
                                        inc="PE" if sl == 3 else None)
                idx += 1
            for sl in range(4):
                s = s4 * 4 + sl
                sl2 = slice((s % 2) * 512, (s % 2) * 512 + 512)
                p.wait("vector", "PE", slab_pe[idx - 1])
                if s == 0:
                    p.wait("vector", "Xt", ev_xall)
                if s >= 2:
                    p.wait("vector", "Go%d" % (s % 2), ev_store[s - 2])
                p.op("vector", "tensor_tensor", sb.t_f32[:, sl2], pbank[sl][:, :], sb.rms_bs[:, :], ALU.mult)
                ev_evac[s] = p.op("vector", "scalar_tensor_tensor", sb.xo[:, sl2], sb.t_f32[:, sl2], 0.5,
                                  sb.xbuf[:, s * 512:(s + 1) * 512], ALU.mult, ALU.add, inc="DV")
                p.wait("gpsimd", "DV", ev_evac[s])
                ev_store[s] = p.dma("gpsimd", x_out_dram[s * 128:(s + 1) * 128, ctt], sb.xo[:, sl2],
                                    "Go%d" % (s % 2))
        if dbg is not None and tt == 0:
            p.gate()
            p.dma("gpsimd", dbg["h2"].ap(), sb.hbuf[:, :], "Gd")
            p.wait("gpsimd", "Gd", p.ctr["Gd"])
            p.gate()
        p.wait("gpsimd", "Go0", ev_store[14])
        p.wait("gpsimd", "Go1", ev_store[15])
    p.barrier()



def emit_attn(p, sb):
    qk_psums = [sb.pA, sb.pB, sb.pC, sb.pD]
    pending_lazy = [None]
    ev_norm = [None, None]
    for hg in range(NH // 2):
        p.gate()
        base = hg * 256
        p.dma("sync", sb.wstream[:, 0:4096].rearrange("p (c o) -> p c o", c=16),
              sb.WqkvT[:, base:base + 256].rearrange("(c p) o -> p c o", p=128), "Wq")
        p.dma("sync", sb.wstream[:, 4096:8192].rearrange("p (c o) -> p c o", c=16),
              sb.WqkvT[:, D + base:D + base + 256].rearrange("(c p) o -> p c o", p=128), "Wq")
        ev_wv = p.dma("sync", sb.wstream[:, 8192:12288].rearrange("p (c o) -> p c o", c=16),
                      sb.WqkvT[:, 2 * D + base:2 * D + base + 256].rearrange("(c p) o -> p c o", p=128), "Wq")
        p.wait("tensor", "Wq", ev_wv)

        ev_chunk = {}
        ev_qkevac = None
        for tg in range(4):
            ev_mm = None
            for c in range(16):
                i = tg * 16 + c
                lbuf = i % 4
                lsl = sb.lnsl[:, lbuf * 512:lbuf * 512 + 512]
                if i >= 4:
                    p.wait("sync", "PE", ev_chunk[i - 4])
                ev_ln = p.dma("sync", lsl, sb.ln2T[c * 128:(c + 1) * 128, tg * 512:(tg + 1) * 512],
                              "Wl%d" % lbuf)
                p.wait("tensor", "Wl%d" % lbuf, ev_ln)
                if c == 0 and tg > 0:
                    p.wait("tensor", "DV", ev_qkevac)
                for sl in range(4):
                    wofs = [0, 4096, 0, 4096][sl] + c * 256 + (sl // 2) * 128
                    p.op("tensor", "matmul", qk_psums[sl][:, :], sb.wstream[:, wofs:wofs + 128],
                         lsl, start=(c == 0), stop=(c == 15))
                for ts in range(4):
                    vp = [sb.pE, sb.pF][ts // 2][:, (ts % 2) * 256:(ts % 2) * 256 + 256]
                    ev_mm = p.op("tensor", "matmul", vp,
                                 lsl[:, ts * 128:ts * 128 + 128],
                                 sb.wstream[:, 8192 + c * 256:8192 + c * 256 + 256],
                                 start=(c == 0 and ts % 2 == 0), stop=(c == 15),
                                 inc="PE" if ts == 3 else None)
                ev_chunk[i] = ev_mm
            p.wait("vector", "PE", ev_mm)
            for sl in range(4):
                p.op("vector", "tensor_copy",
                     sb.lnbuf[:, sl * 2048 + tg * 512:sl * 2048 + (tg + 1) * 512], qk_psums[sl][:, :])
            ev_qkevac = None
            for ts in range(4):
                tsub = tg * 4 + ts
                pp = [sb.pE, sb.pF][ts // 2]
                for j in range(2):
                    ev_qkevac = p.op("vector", "tensor_copy",
                                     sb.vbuf[:, j * 2048 + tsub * 128:j * 2048 + (tsub + 1) * 128],
                                     pp[:, (ts % 2) * 256 + j * 128:(ts % 2) * 256 + j * 128 + 128],
                                     inc="DV" if (ts == 3 and j == 1) else None)
        p.gate()

        for j in range(2):
            if pending_lazy[0] is not None:
                fn = pending_lazy[0]
                pending_lazy[0] = None
                fn()
            p.gate(engines=("scalar", "vector", "tensor"))
            h = 2 * hg + j
            Q = sb.lnbuf[:, (2 * j) * 2048:(2 * j) * 2048 + 2048]
            K = sb.lnbuf[:, (2 * j + 1) * 2048:(2 * j + 1) * 2048 + 2048]
            V = sb.vbuf[:, j * 2048:(j + 1) * 2048]
            for src, dst in ((Q, sb.r_qn2), (K, sb.r_kn2)):
                for i4 in range(4):
                    ssl = slice((i4 % 2) * 512, (i4 % 2) * 512 + 512)
                    if True:
                        pass
                    ev_sq = p.op("scalar", "activation", sb.pe_scr[:, ssl], src[:, i4 * 512:(i4 + 1) * 512],
                                 AF.Square, inc="AC")
                    p.wait("tensor", "AC", ev_sq)
                    ev_n = p.op("tensor", "matmul", sb.pR2[0:1, :], sb.ones_col[:, :], sb.pe_scr[:, ssl],
                                start=True, stop=True, inc="PE")
                    p.wait("vector", "PE", ev_n)
                    ev_cp = p.op("vector", "tensor_copy", dst[0:1, i4 * 512:(i4 + 1) * 512],
                                 sb.pR2[0:1, :], inc="DV")
                    p.wait("scalar", "DV", ev_cp)
            p.op("vector", "tensor_reduce", sb.r_sc1[0:1, 0:1], sb.r_qn2[0:1, :], AXX, ALU.max)
            p.op("vector", "tensor_reduce", sb.r_sc2[0:1, 0:1], sb.r_kn2[0:1, :], AXX, ALU.max)
            for _ in range(8):
                p.op("vector", "engine_nop")
            ev_red = p.op("vector", "engine_nop", inc="DV")
            p.wait("gpsimd", "DV", ev_red)
            p.op("gpsimd", "tensor_add", sb.r_sc3[0:1, 0:1], sb.r_sc1[0:1, 0:1], sb.r_sc2[0:1, 0:1])
            ev_nb = p.op("gpsimd", "tensor_scalar_mul", sb.r_sc3[0:1, 0:1], sb.r_sc3[0:1, 0:1],
                         -SM_SCALE / 2, inc="GP")
            p.wait("tensor", "GP", ev_nb)
            ev_nbc = p.op("tensor", "matmul", sb.pC[:, 0:1], sb.ones_rowf[:, :], sb.r_sc3[0:1, 0:1],
                          start=True, stop=True, inc="PE")
            p.wait("vector", "PE", ev_nbc)
            ev_nbcol = p.op("vector", "tensor_copy", sb.nbcol[:, :], sb.pC[:, 0:1], inc="DV")
            p.wait("scalar", "DV", ev_nbcol)
            p.wait("tensor", "DV", ev_nbcol)

            ev_exp_prev_qg = None
            ev_stage_prev = [None]
            for qg in range(4):
                nvis = 4 * qg + 4
                ev_exp = [None] * 64
                ev_pt = [None] * 64
                ev_lmm = [None] * 64
                banks = [sb.pA, sb.pB, sb.pD, sb.pF, sb.pE]

                def emit_pv(kc, last):
                    sem, val = ev_pt[kc]
                    p.wait("tensor", sem, val)
                    p.op("tensor", "matmul", sb.pC[:, :], V[:, kc * 128:(kc + 1) * 128],
                         sb.ptbuf[:, (kc % 4) * 512:(kc % 4) * 512 + 512],
                         start=(kc == 0), stop=last)
                    ev_lmm[kc] = p.op("tensor", "matmul", sb.pR1[0:1, :], sb.ones_col[:, :],
                                      sb.ptbuf[:, (kc % 4) * 512:(kc % 4) * 512 + 512],
                                      start=(kc == 0), stop=last, inc="PE")

                for kc in range(nvis):
                    pb = (kc + qg) % 5
                    psc = banks[pb]
                    if kc >= 5:
                        p.wait("tensor", "AC", ev_exp[kc - 5])
                    elif ev_exp_prev_qg:
                        p.wait("tensor", "AC", ev_exp_prev_qg)
                    if kc == 4 and qg >= 1 and ev_norm[(qg - 1) % 2] is not None:
                        p.wait("tensor", "DV", ev_norm[(qg - 1) % 2])
                    ev_sc = p.op("tensor", "matmul", psc[:, :], K[:, kc * 128:(kc + 1) * 128],
                                 Q[:, qg * 512:(qg + 1) * 512], start=True, stop=True, inc="PE")
                    p.wait("scalar", "PE", ev_sc)
                    diag = kc - 4 * qg
                    sl4 = kc % 4
                    if diag >= 0:
                        if kc >= 4:
                            p.wait("scalar", ev_pt[kc - 4][0], ev_pt[kc - 4][1])
                        ev_e = p.op("scalar", "activation", sb.pe_scr[:, sl4 * 512:sl4 * 512 + 512],
                                    psc[:, :], AF.Exp, bias=sb.nbcol[:, :], scale=SM_SCALE, inc="AC")
                        ev_exp[kc] = ev_e
                        p.wait("vector", "AC", ev_e)
                        if kc >= 4:
                            p.wait("vector", "PE", ev_lmm[kc - 4])
                        ev_pt[kc] = ("DV", p.op("vector", "tensor_tensor",
                                                sb.ptbuf[:, sl4 * 512:sl4 * 512 + 512],
                                                sb.pe_scr[:, sl4 * 512:sl4 * 512 + 512],
                                                sb.maskbuf[:, diag * 512:(diag + 1) * 512],
                                                ALU.mult, inc="DV"))
                    else:
                        if kc >= 4:
                            p.wait("scalar", "PE", ev_lmm[kc - 4])
                        ev_e = p.op("scalar", "activation", sb.ptbuf[:, sl4 * 512:sl4 * 512 + 512],
                                    psc[:, :], AF.Exp, bias=sb.nbcol[:, :], scale=SM_SCALE, inc="AC")
                        ev_exp[kc] = ev_e
                        ev_pt[kc] = ("AC", ev_e)
                    if kc >= 2 and pending_lazy[0] is not None:
                        fn = pending_lazy[0]
                        pending_lazy[0] = None
                        fn()
                    if kc >= 3:
                        if kc == 3 and ev_stage_prev[0] is not None:
                            p.wait("tensor", "DV", ev_stage_prev[0])
                        emit_pv(kc - 3, last=False)
                for tail in (3, 2, 1):
                    emit_pv(nvis - tail, last=(tail == 1))
                ev_l = ev_lmm[nvis - 1]
                ev_exp_prev_qg = ev_exp[nvis - 1]
                stage = sb.t_bf[:, (qg % 2) * 512:(qg % 2) * 512 + 512]
                p.wait("vector", "PE", ev_l)
                if ev_norm[qg % 2] is not None:
                    p.wait("vector", "DV", ev_norm[qg % 2])
                p.op("vector", "tensor_copy", stage, sb.pC[:, :])
                p.op("vector", "tensor_copy", sb.r_l[0:1, :], sb.pR1[0:1, :])
                ev_stage = p.op("vector", "engine_nop", inc="DV")
                ev_stage_prev[0] = ev_stage

                def lazy(qg=qg, h=h, stage=stage):
                    p.op("vector", "reciprocal", sb.r_rl[0:1, :], sb.r_l[0:1, :])
                    for _ in range(8):
                        p.op("vector", "engine_nop")
                    ev_rl = p.op("vector", "engine_nop", inc="DV")
                    p.wait("tensor", "DV", ev_rl)
                    rlbank = banks[qg % 5]
                    ev_rlbc = p.op("tensor", "matmul", rlbank[:, :], sb.ones_rowf[:, :],
                                   sb.r_rl[0:1, :], start=True, stop=True, inc="PE")
                    p.wait("vector", "PE", ev_rlbc)
                    p.op("vector", "tensor_copy", sb.rls[:, :], rlbank[:, :])
                    ev_norm[qg % 2] = p.op("vector", "tensor_tensor",
                                           sb.hbuf[:, h * 2048 + qg * 512:h * 2048 + (qg + 1) * 512],
                                           stage, sb.rls[:, :], ALU.mult, inc="DV")
                pending_lazy[0] = lazy

    if pending_lazy[0] is not None:
        fn = pending_lazy[0]
        pending_lazy[0] = None
        fn()
    pbank = [sb.pA, sb.pB, sb.pC, sb.pD]
    for tt in range(NTT):
        ctt = slice(tt * 512, (tt + 1) * 512)
        for s4 in range(4):
            p.gate()
            ev_x1 = p.dma("scalar", sb.xbuf[:, 0:2048].rearrange("p (s f) -> p s f", s=4),
                          sb.x1T[s4 * 512:(s4 + 1) * 512, ctt].rearrange("(s p) f -> p s f", p=128),
                          "Xt")
            slab_pe = {}
            for ac in range(16):
                wbuf = ac % 8
                slab = sb.wstream[:, wbuf * 512:wbuf * 512 + 512]
                if ac >= 8:
                    p.wait("sync", "PE", slab_pe[ac - 8])
                ev_w = p.dma("sync", slab, sb.WoT[ac * 128:(ac + 1) * 128, s4 * 512:s4 * 512 + 512],
                             "Wp%d" % wbuf)
                p.wait("tensor", "Wp%d" % wbuf, ev_w)
                for sl in range(4):
                    slab_pe[ac] = p.op("tensor", "matmul", pbank[sl][:, :],
                                       slab[:, sl * 128:sl * 128 + 128],
                                       sb.hbuf[:, ac * 2048 + tt * 512:ac * 2048 + (tt + 1) * 512],
                                       start=(ac == 0), stop=(ac == 15),
                                       inc="PE" if sl == 3 else None)
            ev_store = {}
            for sl in range(4):
                s = s4 * 4 + sl
                sl2 = slice((sl % 2) * 512, (sl % 2) * 512 + 512)
                p.wait("vector", "PE", slab_pe[15])
                if sl == 0:
                    p.wait("vector", "Xt", ev_x1)
                if sl >= 2:
                    p.wait("vector", "Go%d" % (sl % 2), ev_store[sl - 2])
                ev_e = p.op("vector", "tensor_tensor", sb.xo[:, sl2], pbank[sl][:, :],
                            sb.xbuf[:, sl * 512:(sl + 1) * 512], ALU.add, inc="DV")
                p.wait("gpsimd", "DV", ev_e)
                ev_store[sl] = p.dma("gpsimd", sb.x2T[s * 128:(s + 1) * 128, ctt], sb.xo[:, sl2],
                                     "Go%d" % (sl % 2))
            p.wait("gpsimd", "Go0", ev_store[2])
            p.wait("gpsimd", "Go1", ev_store[3])
    p.barrier()



def build_nc(debug_outputs=()):
    nc = bass.Bass("TRN2", target_bir_lowering=False, detect_race_conditions=False)

    class Ctx:
        pass

    sb = Ctx()
    sb.x0T = nc.dram_tensor("x0T", [D, S], F32, kind="ExternalInput")
    sb.wgu_sh = nc.dram_tensor("wgu_sh", [256, 2 * HF], F32, kind="ExternalInput")
    sb.wp_sh = nc.dram_tensor("wp_sh", [1024, D], F32, kind="ExternalInput")
    sb.wqkv_sh = nc.dram_tensor("wqkv_sh", [256, 3 * D], F32, kind="ExternalInput")
    sb.wo_sh = nc.dram_tensor("wo_sh", [256, D], F32, kind="ExternalInput")
    sb.yT = nc.dram_tensor("yT", [D, S], F32, kind="ExternalOutput")

    def internal(name, shape, dt):
        kind = "ExternalOutput" if name in debug_outputs else "Internal"
        return nc.dram_tensor(name, shape, dt, kind=kind)

    sb.q_wgu_slab = [internal(f"q_wgu_s{k}", [256, 4096], BF16) for k in range(4)]
    sb.WguT_slab = [internal(f"WguT_s{k}", [D, 4096], BF16) for k in range(4)]
    sb.q_wp_sh = internal("q_wp_sh", [1024, D], BF16)
    sb.b_wqkv_sh = internal("b_wqkv_sh", [256, 3 * D], BF16)
    sb.b_wo_sh = internal("b_wo_sh", [256, D], BF16)

    sb.WpT = internal("WpT", [HF, D], BF16)
    sb.WqkvT = internal("WqkvT", [D, 3 * D], BF16)
    sb.WoT = internal("WoT", [D, D], BF16)
    sb.dbg_h = internal("dbg_h", [128, 32768], BF16)
    sb.dbg_h2 = internal("dbg_h2", [128, 32768], BF16)
    sb.dbg_ln = internal("dbg_ln", [128, 8192], BF16)
    sb.dbg_rms = internal("dbg_rms", [128, 512], F32)
    sb.x1T = internal("x1T", [D, S], F32)
    sb.x2T = internal("x2T", [D, S], F32)
    sb.ln1T = internal("ln1T", [D, S], BF16)
    sb.ln2T = internal("ln2T", [D, S], BF16)
    sb.ln3T = internal("ln3T", [D, S], BF16)

    selb_np = np.zeros((128, 2), dtype=BF)
    selb_np[:64, 0] = 1
    selb_np[64:, 1] = 1
    selk2_np = np.zeros((2, 128), dtype=np.float32)
    selk2_np[0, :64] = 1
    selk2_np[1, 64:] = 1
    ones_col_np = np.ones((128, 1), dtype=BF)
    ones_rowf_np = np.ones((1, 128), dtype=np.float32)
    masks_np = np.zeros((4 * 128, 512), dtype=BF)
    for jj in range(4):
        for pp in range(128):
            masks_np[jj * 128 + pp, 128 * jj + pp:] = 1
    C_SELB = nc.inline_tensor(selb_np, "c_selb")
    C_SELK2 = nc.inline_tensor(selk2_np, "c_selk2")
    C_ONESC = nc.inline_tensor(ones_col_np, "c_onesc")
    C_ONESRF = nc.inline_tensor(ones_rowf_np, "c_onesrf")
    C_MASKS = nc.inline_tensor(masks_np, "c_masks")

    with ExitStack() as stack:
        def sbuf(name, shape, dt):
            return stack.enter_context(nc.sbuf_tensor(name, shape, dt))

        def psum(name, shape, dt):
            return stack.enter_context(nc.psum_tensor(name, shape, dt))

        sb.lnbuf = sbuf("lnbuf", [128, 8192], BF16)
        sb.hbuf = sbuf("hbuf", [128, 32768], BF16)
        sb.xbuf = sbuf("xbuf", [128, 8192], F32)
        sb.wstream = sbuf("wstream", [128, 16384], BF16)
        sb.vbuf = sbuf("vbuf", [128, 4096], BF16)
        sb.sgbuf = sbuf("sgbuf", [128, 1536], F32)
        sb.h2buf = None
        sb.t_bf = sbuf("t_bf", [128, 2048], BF16)
        sb.t_bf2 = sbuf("t_bf2", [128, 2048], BF16)
        sb.t_f32 = sbuf("t_f32", [128, 1024], F32)
        sb.t_lnout = None
        sb.xo = sbuf("xo", [128, 1024], F32)
        sb.rms_bs = sbuf("rms_bs", [128, 512], F32)
        sb.rls = sbuf("rls", [128, 512], F32)
        sb.ptbuf = sbuf("ptbuf", [128, 2048], BF16)
        sb.pe_scr = sbuf("pe_scr", [128, 2048], BF16)
        sb.h2buf = sb.ptbuf
        sb.t_lnout = sb.pe_scr[:, 1024:2048]
        sb.lnsl = sbuf("lnsl", [128, 2048], BF16)
        sb.maskbuf = sbuf("maskbuf", [128, 2048], BF16)
        sb.nbcol = sbuf("nbcol", [128, 1], F32)
        sb.q_rvrow = sbuf("q_rvrow", [2, 512], F32)
        _qsb = sbuf("q_scaleb", [2, 1024], BF16)
        sb.q_scaleb = [_qsb[:, 0:512], _qsb[:, 512:1024]]
        sb.q_scalef = [sb.sgbuf[0:2, 0:512], sb.sgbuf[0:2, 512:1024]]
        sb.q_thr = [sb.sgbuf[0:2, 1024:1536], sb.q_rvrow[0:2, 0:512]]
        sb.r_mean = sbuf("r_mean", [1, 512], F32)
        sb.r_msq = sbuf("r_msq", [1, 512], F32)
        sb.r_rv = sbuf("r_rv", [1, 512], F32)
        sb.r_rstd = sbuf("r_rstd", [1, 512], F32)
        sb.r_tmp = sb.sgbuf[0:1, 1024:1536]
        sb.r_var = sb.r_rstd
        sb.r_nmr = sb.r_rv
        sb.r_var2 = sb.r_tmp
        sb.r_ms = sb.r_var
        sb.r_l = sb.r_mean
        sb.r_rl = sb.r_msq
        sb.q_w32 = [sb.t_f32[:, 0:512], sb.t_f32[:, 512:1024]]
        sb.q_wb = [sb.t_bf[:, 0:512], sb.t_bf[:, 1024:1536]]
        sb.q_d = [sb.t_bf[:, 512:1024], sb.t_bf[:, 1536:2048]]
        sb.q_aw = [sb.t_bf2[:, 0:512], sb.t_bf2[:, 512:1024]]
        sb.q_sgn = [sb.t_lnout[:, 0:512], sb.t_lnout[:, 512:1024]]
        sb.q_m = [sb.ptbuf[:, 0:512], sb.ptbuf[:, 1024:1536]]
        sb.q_out = [sb.ptbuf[:, 512:1024], sb.ptbuf[:, 1536:2048]]
        sb.q_q = [sb.pe_scr[:, 0:512], sb.pe_scr[:, 512:1024]]
        sb.q_what = [sb.lnsl[:, 0:512], sb.lnsl[:, 512:1024]]
        sb.r_qn2 = sb.xbuf[0:1, 0:2048]
        sb.r_kn2 = sb.xbuf[0:1, 2048:4096]
        sb.r_sc1 = sbuf("r_sc1", [1, 1], F32)
        sb.r_sc2 = sbuf("r_sc2", [1, 1], F32)
        sb.r_sc3 = sbuf("r_sc3", [1, 1], F32)
        sb.selb = sbuf("selb", [128, 2], BF16)
        sb.selk2 = sbuf("selk2", [2, 128], F32)
        sb.ones_col = sbuf("ones_col", [128, 1], BF16)
        sb.ones_rowf = sbuf("ones_rowf", [1, 128], F32)
        sb.pA = psum("pA", [128, 512], F32)
        sb.pB = psum("pB", [128, 512], F32)
        sb.pC = psum("pC", [128, 512], F32)
        sb.pD = psum("pD", [128, 512], F32)
        sb.pE = psum("pE", [128, 512], F32)
        sb.pF = psum("pF", [128, 512], F32)
        sb.pR1 = psum("pR1", [2, 512], F32)
        sb.pR2 = psum("pR2", [2, 512], F32)

        sems = {n: stack.enter_context(nc.semaphore("s" + n)) for n in SEMNAMES}
        p = P(sems)

        for dst, src in ((sb.selb, C_SELB), (sb.selk2, C_SELK2),
                         (sb.ones_col, C_ONESC), (sb.ones_rowf, C_ONESRF)):
            p.dma("sync", dst[:, :], src[:, :], "Wc")
        for jj in range(4):
            p.dma("sync", sb.maskbuf[:, jj * 512:(jj + 1) * 512], C_MASKS[jj * 128:(jj + 1) * 128, :], "Wc")
        p.gate()

        cc_target = emit_prep(p, sb)
        emit_ln_pass(p, sb, sb.x0T, sb.ln1T)
        dbg = ({"h": sb.dbg_h, "h2": sb.dbg_h2, "ln": sb.dbg_ln, "rms": sb.dbg_rms}
               if "dbg_h" in debug_outputs else None)
        emit_ffn(p, sb, sb.ln1T, sb.x0T, sb.x1T, dbg=dbg, cc_wait=cc_target)
        emit_ln_pass(p, sb, sb.x1T, sb.ln2T)
        emit_attn(p, sb)
        emit_ln_pass(p, sb, sb.x2T, sb.ln3T)
        emit_ffn(p, sb, sb.ln3T, sb.x2T, sb.yT)

        with nc.Block() as block:
            for ename in ENGINES:
                def make_body(ename):
                    def body(eng):
                        for f in p.ops[ename]:
                            f(eng)
                    return body
                getattr(block, ename)(make_body(ename))

    return nc


def kernel(L0_attn_hidden_states, L0_attn_w_qkv, L0_attn_w_o, L0_ffn_w_gate_up, L0_ffn_w_proj):
    hs = np.asarray(L0_attn_hidden_states, dtype=np.float32)
    w_qkv = np.asarray(L0_attn_w_qkv, dtype=np.float32)
    w_o = np.asarray(L0_attn_w_o, dtype=np.float32)
    w_gu = np.asarray(L0_ffn_w_gate_up, dtype=np.float32)
    w_p = np.asarray(L0_ffn_w_proj, dtype=np.float32)

    nc = build_nc()
    in_maps = []
    for c in range(NC):
        in_maps.append({
            "x0T": np.ascontiguousarray(hs[c].T),
            "wgu_sh": np.ascontiguousarray(w_gu[:, c * 256:(c + 1) * 256].T),
            "wp_sh": np.ascontiguousarray(w_p[:, c * 1024:(c + 1) * 1024].T),
            "wqkv_sh": np.ascontiguousarray(w_qkv[:, c * 256:(c + 1) * 256].T),
            "wo_sh": np.ascontiguousarray(w_o[:, c * 256:(c + 1) * 256].T),
        })
    res = run_bass_kernel_spmd(nc, in_maps, core_ids=list(range(NC)))
    return np.stack([np.ascontiguousarray(r["yT"].T) for r in res.results])
